# version 06
# baseline (speedup 1.0000x reference)
import contextlib
import numpy as np
import ml_dtypes
import concourse.bass as bass
import concourse.mybir as mybir
from concourse.bass_utils import run_bass_kernel_spmd

F32 = mybir.dt.float32
BF16 = mybir.dt.bfloat16
AF = mybir.ActivationFunctionType
OP = mybir.AluOpType
AX = mybir.AxisListType

P = 128
D = 2048
KD = 16
CT = 512
TT = 4
NCH = 8
FKC = 11
NU = 10
NGU = 6
HQN = 4
EPS = 1e-5
RING = 3
SLOT_COLS = 8192

GU_UNIT_COLS = [4096 * (2 if j < 5 else 1) for j in range(NGU)]
WD_UNIT_COLS = FKC * 512
IFG_UNIT_COLS = 3 * KD * P
WO_UNIT_COLS = KD * 512


def _unit_offsets():
    off = {}
    c = 0
    for u in range(NU):
        for j in range(NGU):
            off[("gu", u, j)] = (c, GU_UNIT_COLS[j])
            c += GU_UNIT_COLS[j]
        for hq in range(HQN):
            off[("wd", u, hq)] = (c, WD_UNIT_COLS)
            c += WD_UNIT_COLS
    for et in range(KD):
        off[("ifg", et, 0)] = (c, IFG_UNIT_COLS)
        c += IFG_UNIT_COLS
    for dq in range(4):
        off[("wo", dq, 0)] = (c, WO_UNIT_COLS)
        c += WO_UNIT_COLS
    return off, c


def _plan_units(n_chunks, phases):
    plan = []

    def ffn():
        for u in range(NU):
            for j in range(NGU):
                plan.append(("gu", u, j))
            for hq in range(HQN):
                plan.append(("wd", u, hq))

    for _c in range(n_chunks):
        if phases in ("ffn1", "attn", "full"):
            ffn()
        if phases in ("attn", "full"):
            for et in range(KD):
                plan.append(("ifg", et, 0))
            for dq in range(4):
                plan.append(("wo", dq, 0))
        if phases == "full":
            ffn()
    return plan


def _pack_host(inp):
    def t(a):
        return np.ascontiguousarray(np.asarray(a, dtype=np.float32).T)

    blocks = []
    swg = t(inp["L0_ffn_sw_gate"])
    swu = t(inp["L0_ffn_sw_up"])
    swd = t(inp["L0_ffn_sw_down"])
    mats = [
        (swg[:, :1408], swu[:, :1408], swd[:1408, :]),
        (swg[:, 1408:], swu[:, 1408:], swd[1408:, :]),
    ]
    wg = np.asarray(inp["L0_ffn_w_gate"], dtype=np.float32)
    wu = np.asarray(inp["L0_ffn_w_up"], dtype=np.float32)
    wd = np.asarray(inp["L0_ffn_w_down"], dtype=np.float32)
    for e in range(8):
        mats.append((t(wg[e]), t(wu[e]), t(wd[e])))
    for (g, u_, d_) in mats:
        ag = g.reshape(KD, P, FKC, P).transpose(2, 0, 1, 3)
        au = u_.reshape(KD, P, FKC, P).transpose(2, 0, 1, 3)
        gu = np.stack([ag, au], axis=1)
        blocks.append(gu.transpose(3, 0, 1, 2, 4).reshape(P, -1))
        ad = d_.reshape(FKC, P, HQN, 512).transpose(2, 0, 1, 3)
        blocks.append(ad.transpose(2, 0, 1, 3).reshape(P, -1))

    def proj_arr(wT):
        return wT.reshape(KD, P, KD, P).transpose(2, 0, 1, 3)

    af_ = proj_arr(t(inp["L0_attn_Wf"]))
    ai_ = proj_arr(t(inp["L0_attn_Wi"]))
    ag_ = proj_arr(t(inp["L0_attn_Wg"]))
    ifg = np.stack([af_, ai_, ag_], axis=1)
    blocks.append(ifg.transpose(3, 0, 1, 2, 4).reshape(P, -1))
    woT = t(inp["L0_attn_Wo"])
    ao = woT.reshape(KD, P, 4, 512).transpose(2, 0, 1, 3)
    blocks.append(ao.transpose(2, 0, 1, 3).reshape(P, -1))

    wstream = np.ascontiguousarray(np.concatenate(blocks, axis=1), dtype=np.float32)

    gwT = t(inp["L0_ffn_gate_weight"])
    gw_arr = gwT.reshape(KD, P, 8).transpose(1, 0, 2).reshape(P, KD * 8)
    cb = np.ascontiguousarray(
        np.concatenate([np.eye(P, dtype=np.float32),
                        np.ones((P, P), dtype=np.float32),
                        gw_arr], axis=1))
    wn = np.asarray(inp["L0_attn_w_norm"], dtype=np.float32).reshape(KD, P).T
    wn = np.ascontiguousarray(
        np.concatenate([wn, np.full((P, 1), EPS, dtype=np.float32)], axis=1))
    return wstream, cb, wn


class Sched:
    OWNER = {"pe": "pe", "act": "act", "dve": "dve"}

    def __init__(self):
        self.q = {k: [] for k in ("pe", "act", "dve", "gp", "sp")}
        self.cnt = {}
        self.n = 0

    def op(self, eng, fn, waits=(), inc=None, inc_by=1):
        ev = None
        if inc is not None:
            self.cnt[inc] = self.cnt.get(inc, 0) + inc_by
            ev = (inc, self.cnt[inc])
        wl = [w for w in waits if w is not None and self.OWNER.get(w[0]) != eng]
        self.q[eng].append((wl, fn, ev, inc_by))
        self.n += 1
        return ev

    def selfwait(self, eng, ev):
        self.q[eng].append(([ev], None, None, 1))

    AUTO_SELFWAIT = ("dve", "act")

    def replay(self, name, eng, sems):
        hwm = {}
        auto = name in self.AUTO_SELFWAIT
        for wl, fn, ev, inc_by in self.q[name]:
            for (s, v) in wl:
                if hwm.get(s, 0) < v:
                    eng.wait_ge(sems[s], v)
                    hwm[s] = v
            if fn is None:
                continue
            ins = fn(eng)
            if ev is not None:
                ins.then_inc(sems[ev[0]], inc_by)
                if auto:
                    eng.wait_ge(sems[ev[0]], ev[1])
                    hwm[ev[0]] = ev[1]


class Stream:

    def __init__(self, S, wsrc, wscr, slots, offsets, plan):
        self.S = S
        self.wsrc = wsrc
        self.wscr = wscr
        self.slots = slots
        self.off = offsets
        self.plan = plan
        self.fetched = 0
        self.consumed = 0
        self.use_ev = [None] * len(plan)
        self.wb_read_ev = [None] * len(plan)
        self.ready = [None] * len(plan)
        self.wb_of_key = {}

    def _fetch_one(self):
        g = self.fetched
        if g >= len(self.plan):
            return
        key = self.plan[g]
        co, ncols = self.off[key]
        slot = self.slots[g % RING]
        war = []
        if g >= RING:
            war.append(self.use_ev[g - RING])
            war.append(self.wb_read_ev[g - RING])
        dst = slot[:, 0:ncols]
        if key not in self.wb_of_key:
            src = self.wsrc[:, co:co + ncols]
            ev = self.S.op("gp", lambda e, s=src, d=dst: e.dma_start(out=d, in_=s),
                           waits=war, inc="wdma", inc_by=16)
            sdst = self.wscr[:, co:co + ncols]
            wb = self.S.op("sp", lambda e, s=dst, d=sdst: e.dma_start(out=d, in_=s),
                           waits=[ev], inc="wb", inc_by=16)
            self.wb_of_key[key] = wb
            self.wb_read_ev[g] = wb
        else:
            src = self.wscr[:, co:co + ncols]
            ev = self.S.op("sp", lambda e, s=src, d=dst: e.dma_start(out=d, in_=s),
                           waits=war + [self.wb_of_key[key]], inc="wb", inc_by=16)
        self.ready[g] = ev
        self.fetched += 1

    def prime(self):
        for _ in range(RING):
            self._fetch_one()

    def consume(self, key):
        g = self.consumed
        assert self.plan[g] == key, (self.plan[g], key)
        return g, self.slots[g % RING], self.ready[g]

    def release(self, g, last_use_ev):
        assert g == self.consumed
        self.use_ev[g] = last_use_ev
        self.consumed += 1
        self._fetch_one()


def build_program(n_chunks=NCH, phases="full", debug=False, LN_UPTO=3):
    offsets, total_cols = _unit_offsets()
    nc = bass.Bass("TRN2", target_bir_lowering=False)

    x_d = nc.dram_tensor("x", [NCH * CT, D], F32, kind="ExternalInput").ap()
    ws_d = nc.dram_tensor("wstream", [P, total_cols], F32, kind="ExternalInput").ap()
    cb_d = nc.dram_tensor("cb", [P, 3 * P], F32, kind="ExternalInput").ap()
    wn_d = nc.dram_tensor("wn", [P, KD + 1], F32, kind="ExternalInput").ap()
    y_d = nc.dram_tensor("y", [NCH * CT, D], F32, kind="ExternalOutput").ap()
    wscr_d = nc.dram_tensor("wscratch", [P, total_cols], BF16, kind="Internal").ap()
    if debug:
        dbg_xnT = nc.dram_tensor("dbg_xnT", [P, KD, CT], BF16, kind="ExternalOutput").ap()
        dbg_cw = nc.dram_tensor("dbg_cw", [P, TT, 8], F32, kind="ExternalOutput").ap()
        dbg_rt = nc.dram_tensor("dbg_rt", [P, TT, 8], F32, kind="ExternalOutput").ap()
        dbg_rs = nc.dram_tensor("dbg_rs", [P, TT, 8], F32, kind="ExternalOutput").ap()

    x_r = x_d.rearrange("(c tt p) h -> c p tt h", c=NCH, tt=TT, p=P)
    y_r = y_d.rearrange("(c tt p) h -> c p tt h", c=NCH, tt=TT, p=P)

    S = Sched()
    es = contextlib.ExitStack()
    with es:
        def sb(name, shape, dt):
            return es.enter_context(nc.sbuf_tensor(name, shape, dt))

        rbuf = [sb("rbuf0", [P, TT * D], F32), sb("rbuf1", [P, TT * D], F32)]
        xnT = sb("xnT", [P, KD, CT], BF16)
        ghat = sb("ghat", [P, FKC, CT], BF16)
        xn_tm = sb("xn_tm", [P, 2, D], BF16)
        sgr = sb("sgr", [P, 2, CT], BF16)
        uprot = sb("uprot", [P, 2, CT], BF16)
        xgy = sb("xgy", [P, KD, CT], BF16)
        a_rot = sb("a_rot", [P, 2, CT], F32)
        ona_rot = sb("ona_rot", [P, 2, CT], F32)
        sgi_rot = sb("sgi_rot", [P, 2, CT], F32)
        si_rot = sb("si_rot", [P, 2, CT], F32)
        b_rot = sb("b_rot", [P, 2, CT], F32)
        h_rot = sb("h_rot", [P, 2, CT], F32)
        sgw_rot = sb("sgw_rot", [P, 2, CT], F32)
        carry = sb("carry", [P, KD], F32)
        r_tmp = sb("r_tmp", [P, CT], F32)
        r_b = sb("r_b", [P, CT], F32)
        rt = sb("rt", [P, TT, 8], F32)
        rex = sb("rex", [P, TT, 8], F32)
        rs = sb("rs", [P, TT, 8], F32)
        rmask = sb("rmask", [P, TT, 8], F32)
        cw = sb("cw", [P, TT, 8], F32)
        rmx = sb("rmx", [P, TT, 1], F32)
        rnmx = sb("rnmx", [P, TT, 1], F32)
        rm1 = sb("rm1", [P, TT, 1], F32)
        rm2 = sb("rm2", [P, TT, 1], F32)
        rz = sb("rz", [P, TT, 1], F32)
        rzi = sb("rzi", [P, TT, 1], F32)
        rden = sb("rden", [P, TT, 1], F32)
        rdi = sb("rdi", [P, TT, 1], F32)
        ln_bnst = sb("ln_bnst", [P, TT, 4, 6], F32)
        ln_mv = sb("ln_mv", [P, TT, 2], F32)
        ln_t = sb("ln_t", [P, TT, 1], F32)
        ln_rstd = sb("ln_rstd", [P, TT, 1], F32)
        ln_nmr = sb("ln_nmr", [P, TT, 1], F32)
        id_sb = sb("id_sb", [P, P], BF16)
        ones_sb = sb("ones_sb", [P, P], BF16)
        gw_sb = sb("gw_sb", [P, P], BF16)
        wn_sb = sb("wn_sb", [P, KD + 1], F32)
        slots = [sb(f"wslot{i}", [P, SLOT_COLS], BF16) for i in range(RING)]

        ps = [es.enter_context(nc.psum_tensor(f"ps{i}", [P, CT], F32)) for i in range(6)]
        ps_trb = [es.enter_context(nc.psum_tensor(f"ps_tr{i}", [P, 8, P], BF16))
                  for i in range(2)]

        sem_names = ["xin", "xout", "wdma", "wb", "const", "pe", "act", "dve"]
        sems = {n: es.enter_context(nc.semaphore("s_" + n)) for n in sem_names}

        stream = Stream(S, ws_d, wscr_d, slots, offsets, _plan_units(n_chunks, phases))

        S.op("gp", lambda e: e.dma_start(out=id_sb[:], in_=cb_d[:, 0:P]),
             inc="const", inc_by=16)
        S.op("gp", lambda e: e.dma_start(out=ones_sb[:], in_=cb_d[:, P:2 * P]),
             inc="const", inc_by=16)
        S.op("gp", lambda e: e.dma_start(out=gw_sb[:], in_=cb_d[:, 2 * P:3 * P]),
             inc="const", inc_by=16)
        S.op("sp", lambda e: e.dma_start(out=wn_sb[:], in_=wn_d),
             inc="const", inc_by=16)
        stream.prime()
        const_ev = ("const", S.cnt["const"])

        st = {
            "resid_dve": None,
            "xnT_evts": [],
            "sq_act": None,
            "tm_tr": [None, None],
                        "bank": [None] * 6,
            "tr_evac": [None, None],
            "cw_ev": None,
            "tr_bnk": 0,
            "ln_apply": {},
            "xgy_pe": None,
            "sgr_dve": [None, None],
            "up_dve": [None, None],
            "ghat_pe": None,
            "a_read": [None, None], "ona_read": [None, None],
            "sgi_read": [None, None], "sgw_read": [None, None],
            "h_read": [None, None], "b_read": [None, None], "si_read": [None, None],
        }

        def emit_ln_q(c, site, tt, q, RS):
            deps = []
            if site == 1:
                deps.append(("xin", 16 * (c + 1)))
            S.op("dve", lambda e, tt=tt, q=q, RS=RS: e.bn_stats(
                ln_bnst[:, tt, q, :], RS[:, tt, q * 512:(q + 1) * 512]),
                waits=deps, inc="dve")

        def emit_ln_fin(c, site, tt, RS):
            S.op("dve", lambda e, tt=tt: e.bn_aggr(
                ln_mv[:, tt, :], ln_bnst[:, tt].rearrange("p a b -> p (a b)")),
                inc="dve")
            ev_d = ("dve", S.cnt["dve"])
            S.op("act", lambda e, tt=tt: e.activation(
                ln_t[:, tt, :], ln_mv[:, tt, 1:2], AF.Ln, bias=wn_sb[:, KD:KD + 1]),
                waits=[ev_d], inc="act")
            ev_r = S.op("act", lambda e, tt=tt: e.activation(
                ln_rstd[:, tt, :], ln_t[:, tt, :], AF.Exp, scale=-0.5), inc="act")
            ev_nmr = S.op("dve", lambda e, tt=tt: e.tensor_scalar(
                ln_nmr[:, tt, :], ln_mv[:, tt, 0:1], ln_rstd[:, tt, :], -1.0,
                OP.mult, OP.mult),
                waits=[ev_r], inc="dve")
            st["ln_apply"][(site, tt)] = ev_nmr

        def emit_ln_stats_tt(c, site, tt, RS):
            for q in range(4):
                emit_ln_q(c, site, tt, q, RS)
            emit_ln_fin(c, site, tt, RS)

        def emit_ln(c, site, upto=3):
            RS = st["RS"]
            evts = []
            for tt in range(TT):
                if (site, tt) not in st["ln_apply"]:
                    emit_ln_stats_tt(c, site, tt, RS)
                ev_nmr = st["ln_apply"].pop((site, tt))
                par = tt % 2
                ev_ap = S.op("act", lambda e, tt=tt, par=par, RS=RS: e.activation(
                    xn_tm[:, par, :], RS[:, tt, :], AF.Identity,
                    bias=ln_nmr[:, tt, :], scale=ln_rstd[:, tt, :]),
                    waits=[ev_nmr, st["tm_tr"][par], st["resid_dve"]], inc="act")
                for batch in range(KD // 8):
                    bnk = st["tr_bnk"]
                    st["tr_bnk"] = 1 - bnk
                    for j in range(8):
                        k = batch * 8 + j
                        w0 = [ev_ap, const_ev]
                        if j == 0:
                            w0.append(st["tr_evac"][bnk])
                        ev_tr = S.op("pe", lambda e, k=k, j=j, bnk=bnk, par=par:
                                     e.transpose(ps_trb[bnk][:, j, :],
                                                 xn_tm[:, par, k * P:(k + 1) * P],
                                                 id_sb[:]),
                                     waits=w0, inc="pe" if j == 7 else None)
                    st["tm_tr"][par] = ev_tr
                    for j in range(8):
                        k = batch * 8 + j
                        ev_ev = S.op("dve", lambda e, tt=tt, k=k, j=j, bnk=bnk:
                                     e.tensor_copy(xnT[:, k, tt * P:(tt + 1) * P],
                                                   ps_trb[bnk][:, j, :]),
                                     waits=[ev_tr] if j == 0 else [], inc="dve")
                    st["tr_evac"][bnk] = ev_ev
                    evts.append(ev_ev)
            st["xnT_evts"] = [evts[-1]]

        def emit_router(c, call):
            for tt in range(TT):
                bk = 2 + tt
                for k in range(KD):
                    S.op("pe", lambda e, tt=tt, k=k, bk=bk: e.matmul(
                        ps[bk][:, 0:8], xnT[:, k, tt * P:(tt + 1) * P],
                        gw_sb[:, k * 8:(k + 1) * 8],
                        start=(k == 0), stop=(k == KD - 1),
                        skip_group_check=True),
                        waits=(st["xnT_evts"] + [st["bank"][bk], const_ev]) if k == 0 else [],
                        inc="pe" if k == KD - 1 else None)
                ev_mm = ("pe", S.cnt["pe"])
                ev_cp = S.op("dve", lambda e, tt=tt, bk=bk: e.tensor_copy(rt[:, tt, :], ps[bk][:, 0:8]),
                             waits=[ev_mm], inc="dve")
                st["bank"][bk] = ev_cp
            ev_r = S.op("dve", lambda e: e.reduce_max(rmx[:], rt[:], axis=AX.X), inc="dve")
            S.selfwait("dve", ev_r)
            S.op("dve", lambda e: e.tensor_single_scalar(rnmx[:], rmx[:], -1.0, OP.mult),
                 inc="dve")
            ev_n = ("dve", S.cnt["dve"])
            exev = []
            for tt in range(TT):
                exev.append(S.op("act", lambda e, tt=tt: e.activation(
                    rex[:, tt, :], rt[:, tt, :], AF.Exp, bias=rnmx[:, tt, :]),
                    waits=[ev_n], inc="act"))
            ev_r = S.op("dve", lambda e: e.reduce_sum(rz[:], rex[:], axis=AX.X),
                 waits=[exev[-1]], inc="dve")
            S.selfwait("dve", ev_r)
            S.op("dve", lambda e: e.reciprocal(rzi[:], rz[:]), inc="dve")
            for tt in range(TT):
                S.op("dve", lambda e, tt=tt: e.tensor_single_scalar(
                    rs[:, tt, :], rex[:, tt, :], rzi[:, tt, :], OP.mult), inc="dve")
            ev_r = S.op("dve", lambda e: e.reduce_max(rm1[:], rs[:], axis=AX.X), inc="dve")
            S.selfwait("dve", ev_r)
            for tt in range(TT):
                S.op("dve", lambda e, tt=tt: e.tensor_single_scalar(
                    rmask[:, tt, :], rs[:, tt, :], rm1[:, tt, :], OP.is_ge), inc="dve")
                S.op("dve", lambda e, tt=tt: e.scalar_tensor_tensor(
                    rex[:, tt, :], rmask[:, tt, :], -1e30, rs[:, tt, :],
                    OP.mult, OP.add), inc="dve")
            ev_r = S.op("dve", lambda e: e.reduce_max(rm2[:], rex[:], axis=AX.X), inc="dve")
            S.selfwait("dve", ev_r)
            for tt in range(TT):
                S.op("dve", lambda e, tt=tt: e.tensor_single_scalar(
                    rmask[:, tt, :], rs[:, tt, :], rm2[:, tt, :], OP.is_ge), inc="dve")
            S.op("dve", lambda e: e.tensor_add(rden[:], rm1[:], rm2[:]), inc="dve")
            S.op("dve", lambda e: e.tensor_single_scalar(rden[:], rden[:], 1e-20, OP.add),
                 inc="dve")
            S.op("dve", lambda e: e.reciprocal(rdi[:], rden[:]), inc="dve")
            S.op("dve", lambda e: e.tensor_single_scalar(rdi[:], rdi[:], 0.5, OP.mult),
                 inc="dve")
            for tt in range(TT):
                S.op("dve", lambda e, tt=tt: e.tensor_mul(
                    rex[:, tt, :], rs[:, tt, :], rmask[:, tt, :]), inc="dve")
                ev = S.op("dve", lambda e, tt=tt: e.tensor_single_scalar(
                    cw[:, tt, :], rex[:, tt, :], rdi[:, tt, :], OP.mult), inc="dve")
            st["cw_ev"] = ev

        def emit_expert(c, call, u, tail_hook=None):
            gh_evts = []
            slot_info = None
            ev_g = {}
            ev_up_copy = {-1: None}
            for fc in range(FKC):
                j = fc // 2
                fl = fc % 2
                if fl == 0:
                    g_id, slot, ready = stream.consume(("gu", u, j))
                    slot_info = (g_id, slot, ready)
                g_id, slot, ready = slot_info
                gb = fc % 2

                def gtile(slot, fl, gu, k):
                    base = (fl * 2 + gu) * 2048 + k * P
                    return slot[:, base:base + P]

                for k in range(KD):
                    w0 = []
                    if k == 0:
                        w0 = st["xnT_evts"] + [ready, st["bank"][0]]
                    S.op("pe", lambda e, slot=slot, fl=fl, k=k: e.matmul(
                        ps[0][:], gtile(slot, fl, 0, k), xnT[:, k, :],
                        start=(k == 0), stop=(k == KD - 1)),
                        waits=w0, inc="pe" if k == KD - 1 else None)
                ev_gate = ("pe", S.cnt["pe"])
                for k in range(KD):
                    w0 = [ev_up_copy[fc - 1], st["bank"][1]] if k == 0 else []
                    S.op("pe", lambda e, slot=slot, fl=fl, k=k: e.matmul(
                        ps[1][:], gtile(slot, fl, 1, k), xnT[:, k, :],
                        start=(k == 0), stop=(k == KD - 1)),
                        waits=w0, inc="pe" if k == KD - 1 else None)
                ev_up = ("pe", S.cnt["pe"])
                if fc == FKC - 1:
                    stream.release(g_id, ev_up)
                elif fl == 1:
                    stream.release(g_id, ev_up)
                ev_si = S.op("act", lambda e, gb=gb: e.activation(
                    sgr[:, gb, :], ps[0][:], AF.Silu),
                    waits=[ev_gate, st["sgr_dve"][gb]], inc="act")
                st["bank"][0] = ev_si
                ev_uc = S.op("act", lambda e, gb=gb: e.copy(uprot[:, gb, :], ps[1][:]),
                             waits=[ev_up, st["up_dve"][gb]], inc="act")
                ev_up_copy[fc] = ev_uc
                st["bank"][1] = ev_uc
                ev_gh = S.op("dve", lambda e, fc=fc, gb=gb: e.tensor_mul(
                    ghat[:, fc, :], sgr[:, gb, :], uprot[:, gb, :]),
                    waits=[ev_si, ev_uc, st["ghat_pe"]], inc="dve")
                st["sgr_dve"][gb] = ev_gh
                st["up_dve"][gb] = ev_gh
                gh_evts.append(ev_gh)
            ghat_ready = gh_evts[-1]

            RS = st["RS"]
            for hq in range(HQN):
                g_id, slot, ready = stream.consume(("wd", u, hq))
                last_grp = None
                for tt in range(TT):
                    bk = 2 + tt
                    for f in range(FKC):
                        w0 = [gh_evts[f]]
                        if f == 0:
                            w0 += [ready, st["bank"][bk]]
                        S.op("pe", lambda e, slot=slot, f=f, tt=tt, bk=bk: e.matmul(
                            ps[bk][:], ghat[:, f, tt * P:(tt + 1) * P],
                            slot[:, f * 512:(f + 1) * 512],
                            start=(f == 0), stop=(f == FKC - 1)),
                            waits=w0, inc="pe" if f == FKC - 1 else None)
                    ev_grp = ("pe", S.cnt["pe"])
                    last_grp = ev_grp
                    if u < 2:
                        cwsc = 0.5
                    else:
                        cwsc = cw[:, tt, (u - 2):(u - 1)]
                    ev_ev = S.op("dve", lambda e, tt=tt, hq=hq, bk=bk, cwsc=cwsc, RS=RS:
                                 e.scalar_tensor_tensor(
                                     RS[:, tt, hq * 512:(hq + 1) * 512], ps[bk][:],
                                     cwsc, RS[:, tt, hq * 512:(hq + 1) * 512],
                                     OP.mult, OP.add),
                                 waits=[ev_grp, st["cw_ev"]], inc="dve")
                    st["bank"][bk] = ev_ev
                    st["resid_dve"] = ev_ev
                    if tail_hook is not None:
                        tail_hook(tt, hq)
                stream.release(g_id, last_grp)
            st["ghat_pe"] = last_grp

        def emit_hgrn(c, tail_hook=None):
            RS, XG = st["RS"], st["XG"]
            sq_evts = []
            for et in range(KD):
                par = et % 2
                g_id, slot, ready = stream.consume(("ifg", et, 0))
                pe_ev = {}
                for mi, bk in ((0, 0), (1, 1), (2, 2)):
                    for k in range(KD):
                        w0 = []
                        if k == 0:
                            w0 = st["xnT_evts"] + [ready, st["bank"][bk]]
                        S.op("pe", lambda e, slot=slot, mi=mi, k=k, bk=bk: e.matmul(
                            ps[bk][:], slot[:, (mi * KD + k) * P:(mi * KD + k + 1) * P],
                            xnT[:, k, :],
                            start=(k == 0), stop=(k == KD - 1)),
                            waits=w0, inc="pe" if k == KD - 1 else None)
                    pe_ev[mi] = ("pe", S.cnt["pe"])
                stream.release(g_id, pe_ev[2])
                ev_a = S.op("act", lambda e, par=par: e.activation(
                    a_rot[:, par, :], ps[0][:], AF.Sigmoid),
                    waits=[pe_ev[0], st["a_read"][par]], inc="act")
                ev_ona = S.op("act", lambda e, par=par: e.activation(
                    ona_rot[:, par, :], ps[0][:], AF.Sigmoid, scale=-1.0),
                    waits=[st["ona_read"][par]], inc="act")
                st["bank"][0] = ev_ona
                ev_sgi = S.op("act", lambda e, par=par: e.activation(
                    sgi_rot[:, par, :], ps[1][:], AF.Sigmoid),
                    waits=[pe_ev[1], st["sgi_read"][par]], inc="act")
                ev_sgg = S.op("act", lambda e, par=par: e.activation(
                    sgw_rot[:, par, :], ps[2][:], AF.Sigmoid),
                    waits=[pe_ev[2], st["sgw_read"][par]], inc="act")
                ev_si = S.op("dve", lambda e, par=par: e.tensor_mul(
                    si_rot[:, par, :], sgi_rot[:, par, :], ps[1][:]),
                    waits=[ev_sgi, st["si_read"][par]], inc="dve")
                st["sgi_read"][par] = ev_si
                st["bank"][1] = ev_si
                ev_b = S.op("dve", lambda e, par=par: e.tensor_mul(
                    b_rot[:, par, :], si_rot[:, par, :], ona_rot[:, par, :]),
                    waits=[ev_ona, st["b_read"][par]], inc="dve")
                st["si_read"][par] = ev_b
                st["ona_read"][par] = ev_b
                ev_sgw = S.op("dve", lambda e, par=par, et=et: e.scalar_tensor_tensor(
                    sgw_rot[:, par, :], sgw_rot[:, par, :], wn_sb[:, et:et + 1],
                    ps[2][:], OP.mult, OP.mult),
                    waits=[ev_sgg], inc="dve")
                st["bank"][2] = ev_sgw
                init = 0.0 if c == 0 else carry[:, et:et + 1]
                ev_h = S.op("dve", lambda e, par=par, init=init: e.tensor_tensor_scan(
                    h_rot[:, par, :], a_rot[:, par, :], b_rot[:, par, :], init,
                    OP.mult, OP.add),
                    waits=[ev_a, st["h_read"][par]], inc="dve")
                st["a_read"][par] = ev_h
                st["b_read"][par] = ev_h
                S.op("dve", lambda e, et=et, par=par: e.tensor_copy(
                    carry[:, et:et + 1], h_rot[:, par, CT - 1:CT]), inc="dve")
                ev_xg = S.op("dve", lambda e, et=et, par=par, XG=XG: e.tensor_mul(
                    XG[:, et, :], h_rot[:, par, :], sgw_rot[:, par, :]),
                    waits=[st["sq_act"], ("xout", 16 * c)], inc="dve")
                st["h_read"][par] = ev_xg
                st["sgw_read"][par] = ev_xg
                ev_sq = S.op("act", lambda e, et=et, XG=XG: e.activation(
                    xgy[:, et, :], XG[:, et, :], AF.Square),
                    waits=[ev_xg, st["xgy_pe"]], inc="act")
                sq_evts.append(ev_sq)
            st["sq_act"] = sq_evts[-1]

            for et in range(KD):
                S.op("pe", lambda e, et=et: e.matmul(
                    ps[2][:], ones_sb[:], xgy[:, et, :],
                    start=(et == 0), stop=(et == KD - 1)),
                    waits=[sq_evts[et], st["bank"][2]] if et == 0 else [sq_evts[et]],
                    inc="pe" if et == KD - 1 else None)
            ev_ones = ("pe", S.cnt["pe"])
            S.op("dve", lambda e: e.tensor_single_scalar(r_tmp[:], ps[2][:], 1.0 / D, OP.mult),
                 waits=[ev_ones], inc="dve")
            ev_rt = ("dve", S.cnt["dve"])
            st["bank"][2] = ev_rt
            S.op("act", lambda e: e.activation(r_tmp[:], r_tmp[:], AF.Ln, bias=wn_sb[:, KD:KD + 1]),
                 waits=[ev_rt], inc="act")
            ev_rb = S.op("act", lambda e: e.activation(r_b[:], r_tmp[:], AF.Exp, scale=-0.5),
                         inc="act")
            y_evts = []
            for et in range(KD):
                y_evts.append(S.op("dve", lambda e, et=et, XG=XG: e.tensor_mul(
                    xgy[:, et, :], XG[:, et, :], r_b[:]),
                    waits=[ev_rb, ev_ones], inc="dve"))
            st["xload_deps"] = [y_evts[-1], sq_evts[-1]]
            last_grp = None
            for dq in range(4):
                g_id, slot, ready = stream.consume(("wo", dq, 0))
                for et in range(KD):
                    for tt in range(TT):
                        bk = 2 + tt
                        w0 = [y_evts[et]]
                        if et == 0:
                            w0 = [ready, y_evts[et], st["bank"][bk]]
                        S.op("pe", lambda e, slot=slot, et=et, tt=tt, bk=bk: e.matmul(
                            ps[bk][:], xgy[:, et, tt * P:(tt + 1) * P],
                            slot[:, et * 512:(et + 1) * 512],
                            start=(et == 0), stop=(et == KD - 1),
                            skip_group_check=True),
                            waits=w0, inc="pe" if (et == KD - 1) else None)
                        if et == KD - 1:
                            ev_grp = ("pe", S.cnt["pe"])
                            ev_ev = S.op("dve", lambda e, tt=tt, dq=dq, bk=bk, RS=RS:
                                         e.tensor_tensor(
                                             RS[:, tt, dq * 512:(dq + 1) * 512],
                                             RS[:, tt, dq * 512:(dq + 1) * 512],
                                             ps[bk][:], OP.add),
                                         waits=[ev_grp], inc="dve")
                            st["bank"][bk] = ev_ev
                            st["resid_dve"] = ev_ev
                            last_grp = ev_grp
                            if tail_hook is not None:
                                tail_hook(tt, dq)
                stream.release(g_id, last_grp)
            st["xgy_pe"] = last_grp

        st["xload_deps"] = []
        for c in range(n_chunks):
            RS = rbuf[c % 2][:].rearrange("p (a b) -> p a b", a=TT)
            XG = rbuf[1 - (c % 2)][:].rearrange("p (a b) -> p a b", a=KD)
            st["RS"], st["XG"] = RS, XG
            if c == 0 or phases != "full":
                w_load = ([("xout", 16 * c)] if c > 0 else []) + st["xload_deps"]
                S.op("sp", lambda e, c=c, RS=RS: e.dma_start(out=RS, in_=x_r[c]),
                     waits=w_load, inc="xin", inc_by=16)
            if phases == "copy":
                S.op("sp", lambda e, c=c, RS=RS: e.dma_start(out=y_r[c], in_=RS),
                     waits=[("xin", 16 * (c + 1))], inc="xout", inc_by=16)
                continue
            emit_ln(c, 1, upto=LN_UPTO)
            if debug and c == 0 and phases in ("ln1", "ffn1"):
                S.op("sp", lambda e: e.dma_start(out=dbg_xnT, in_=xnT[:]),
                     waits=st["xnT_evts"], inc="xout", inc_by=16)
            if phases in ("ffn1", "attn", "full"):
                emit_router(c, 0)
                if debug and c == 0:
                    S.op("sp", lambda e: e.dma_start(out=dbg_cw, in_=cw[:]),
                         waits=[st["cw_ev"]], inc="xout", inc_by=16)
                def hook2(tt, q, c=c, RS=RS):
                    emit_ln_q(c, 2, tt, q, RS)
                    if q == 3:
                        emit_ln_fin(c, 2, tt, RS)
                if phases not in ("attn", "full"):
                    hook2 = None
                for u in range(NU):
                    emit_expert(c, 0, u, tail_hook=hook2 if u == NU - 1 else None)
            if phases in ("attn", "full"):
                emit_ln(c, 2)
                def hook3(tt, q, c=c, RS=RS):
                    emit_ln_q(c, 3, tt, q, RS)
                    if q == 3:
                        emit_ln_fin(c, 3, tt, RS)
                if phases != "full":
                    hook3 = None
                emit_hgrn(c, tail_hook=hook3)
                if phases == "full" and c + 1 < n_chunks:
                    nxt = rbuf[(c + 1) % 2][:].rearrange("p (a b) -> p a b", a=TT)
                    S.op("sp", lambda e, c=c, nxt=nxt: e.dma_start(
                        out=nxt, in_=x_r[c + 1]),
                        waits=st["xload_deps"], inc="xin", inc_by=16)
            if phases == "full":
                emit_ln(c, 3)
                emit_router(c, 1)
                if c + 1 < n_chunks:
                    RSn = rbuf[(c + 1) % 2][:].rearrange("p (a b) -> p a b", a=TT)
                    def hook1(tt, q, c=c, RSn=RSn):
                        emit_ln_q(c + 1, 1, tt, q, RSn)
                        if q == 3:
                            emit_ln_fin(c + 1, 1, tt, RSn)
                else:
                    hook1 = None
                for u in range(NU):
                    emit_expert(c, 1, u, tail_hook=hook1 if u == NU - 1 else None)
            S.op("sp", lambda e, c=c, RS=RS: e.dma_start(out=y_r[c], in_=RS),
                 waits=[st["resid_dve"], ("dve", S.cnt["dve"])], inc="xout", inc_by=16)

        final_out = ("xout", S.cnt["xout"])
        S.op("sp", None, waits=[final_out])

        with nc.Block() as block:
            @block.sync
            def _(eng):
                S.replay("sp", eng, sems)

            @block.gpsimd
            def _(eng):
                S.replay("gp", eng, sems)

            @block.tensor
            def _(eng):
                S.replay("pe", eng, sems)

            @block.scalar
            def _(eng):
                S.replay("act", eng, sems)

            @block.vector
            def _(eng):
                S.replay("dve", eng, sems)

    return nc


def kernel(L0_attn_x, L0_attn_Wi, L0_attn_Wf, L0_attn_Wg, L0_attn_w_norm,
           L0_attn_Wo, L0_ffn_gate_weight, L0_ffn_w_gate, L0_ffn_w_up,
           L0_ffn_w_down, L0_ffn_sw_gate, L0_ffn_sw_up, L0_ffn_sw_down):
    inp = dict(L0_attn_Wi=L0_attn_Wi, L0_attn_Wf=L0_attn_Wf, L0_attn_Wg=L0_attn_Wg,
               L0_attn_w_norm=L0_attn_w_norm, L0_attn_Wo=L0_attn_Wo,
               L0_ffn_gate_weight=L0_ffn_gate_weight, L0_ffn_w_gate=L0_ffn_w_gate,
               L0_ffn_w_up=L0_ffn_w_up, L0_ffn_w_down=L0_ffn_w_down,
               L0_ffn_sw_gate=L0_ffn_sw_gate, L0_ffn_sw_up=L0_ffn_sw_up,
               L0_ffn_sw_down=L0_ffn_sw_down)
    wstream, cb, wn = _pack_host(inp)
    x = np.asarray(L0_attn_x, dtype=np.float32)
    nc = build_program()
    in_maps = [dict(x=np.ascontiguousarray(x[b]), wstream=wstream, cb=cb, wn=wn)
               for b in range(8)]
    res = run_bass_kernel_spmd(nc, in_maps, core_ids=list(range(8)))
    return np.stack([r["y"] for r in res.results], axis=0)


# version 08
# speedup vs baseline: 1.0129x; 1.0129x over previous
import contextlib
import numpy as np
import ml_dtypes
import concourse.bass as bass
import concourse.mybir as mybir
from concourse.bass_utils import run_bass_kernel_spmd

F32 = mybir.dt.float32
BF16 = mybir.dt.bfloat16
AF = mybir.ActivationFunctionType
OP = mybir.AluOpType
AX = mybir.AxisListType

P = 128
D = 2048
KD = 16
CT = 512
TT = 4
NCH = 8
FKC = 11
NU = 10
NGU = 6
HQN = 4
EPS = 1e-5
RING = 3
SLOT_COLS = 8192

GU_UNIT_COLS = [4096 * (2 if j < 5 else 1) for j in range(NGU)]
WD_UNIT_COLS = FKC * 512
IFG_UNIT_COLS = 3 * KD * P
WO_UNIT_COLS = KD * 512


def _unit_offsets():
    off = {}
    c = 0
    for u in range(NU):
        for j in range(NGU):
            off[("gu", u, j)] = (c, GU_UNIT_COLS[j])
            c += GU_UNIT_COLS[j]
        for hq in range(HQN):
            off[("wd", u, hq)] = (c, WD_UNIT_COLS)
            c += WD_UNIT_COLS
    for et in range(KD):
        off[("ifg", et, 0)] = (c, IFG_UNIT_COLS)
        c += IFG_UNIT_COLS
    for dq in range(4):
        off[("wo", dq, 0)] = (c, WO_UNIT_COLS)
        c += WO_UNIT_COLS
    return off, c


def _plan_units(n_chunks, phases):
    plan = []

    def ffn():
        for u in range(NU):
            for j in range(NGU):
                plan.append(("gu", u, j))
            for hq in range(HQN):
                plan.append(("wd", u, hq))

    for _c in range(n_chunks):
        if phases in ("ffn1", "attn", "full"):
            ffn()
        if phases in ("attn", "full"):
            for et in range(KD):
                plan.append(("ifg", et, 0))
            for dq in range(4):
                plan.append(("wo", dq, 0))
        if phases == "full":
            ffn()
    return plan


def _pack_host(inp):
    def t(a):
        return np.ascontiguousarray(np.asarray(a, dtype=np.float32).T)

    blocks = []
    swg = t(inp["L0_ffn_sw_gate"])
    swu = t(inp["L0_ffn_sw_up"])
    swd = t(inp["L0_ffn_sw_down"])
    mats = [
        (swg[:, :1408], swu[:, :1408], swd[:1408, :]),
        (swg[:, 1408:], swu[:, 1408:], swd[1408:, :]),
    ]
    wg = np.asarray(inp["L0_ffn_w_gate"], dtype=np.float32)
    wu = np.asarray(inp["L0_ffn_w_up"], dtype=np.float32)
    wd = np.asarray(inp["L0_ffn_w_down"], dtype=np.float32)
    for e in range(8):
        mats.append((t(wg[e]), t(wu[e]), t(wd[e])))
    for (g, u_, d_) in mats:
        ag = g.reshape(KD, P, FKC, P).transpose(2, 0, 1, 3)
        au = u_.reshape(KD, P, FKC, P).transpose(2, 0, 1, 3)
        gu = np.stack([ag, au], axis=1)
        blocks.append(gu.transpose(3, 0, 1, 2, 4).reshape(P, -1))
        ad = d_.reshape(FKC, P, HQN, 512).transpose(2, 0, 1, 3)
        blocks.append(ad.transpose(2, 0, 1, 3).reshape(P, -1))

    def proj_arr(wT):
        return wT.reshape(KD, P, KD, P).transpose(2, 0, 1, 3)

    af_ = proj_arr(t(inp["L0_attn_Wf"]))
    ai_ = proj_arr(t(inp["L0_attn_Wi"]))
    ag_ = proj_arr(t(inp["L0_attn_Wg"]))
    ifg = np.stack([af_, ai_, ag_], axis=1)
    blocks.append(ifg.transpose(3, 0, 1, 2, 4).reshape(P, -1))
    woT = t(inp["L0_attn_Wo"])
    ao = woT.reshape(KD, P, 4, 512).transpose(2, 0, 1, 3)
    blocks.append(ao.transpose(2, 0, 1, 3).reshape(P, -1))

    wstream = np.ascontiguousarray(np.concatenate(blocks, axis=1), dtype=np.float32)

    gwT = t(inp["L0_ffn_gate_weight"])
    gw_arr = gwT.reshape(KD, P, 8).transpose(1, 0, 2).reshape(P, KD * 8)
    cb = np.ascontiguousarray(
        np.concatenate([np.eye(P, dtype=np.float32),
                        np.ones((P, P), dtype=np.float32),
                        gw_arr], axis=1))
    wn = np.asarray(inp["L0_attn_w_norm"], dtype=np.float32).reshape(KD, P).T
    wn = np.ascontiguousarray(
        np.concatenate([wn, np.full((P, 1), EPS, dtype=np.float32)], axis=1))
    return wstream, cb, wn


class Sched:
    OWNER = {"pe": "pe", "act": "act", "dve": "dve"}

    def __init__(self):
        self.q = {k: [] for k in ("pe", "act", "dve", "gp", "sp")}
        self.cnt = {}
        self.n = 0

    def op(self, eng, fn, waits=(), inc=None, inc_by=1):
        ev = None
        if inc is not None:
            self.cnt[inc] = self.cnt.get(inc, 0) + inc_by
            ev = (inc, self.cnt[inc])
        wl = [w for w in waits if w is not None and self.OWNER.get(w[0]) != eng]
        self.q[eng].append((wl, fn, ev, inc_by))
        self.n += 1
        return ev

    def selfwait(self, eng, ev):
        self.q[eng].append(([ev], None, None, 1))

    AUTO_SELFWAIT = ("dve", "act")

    def replay(self, name, eng, sems):
        hwm = {}
        auto = name in self.AUTO_SELFWAIT
        for wl, fn, ev, inc_by in self.q[name]:
            for (s, v) in wl:
                if hwm.get(s, 0) < v:
                    eng.wait_ge(sems[s], v)
                    hwm[s] = v
            if fn is None:
                continue
            ins = fn(eng)
            if ev is not None:
                ins.then_inc(sems[ev[0]], inc_by)
                if auto:
                    eng.wait_ge(sems[ev[0]], ev[1])
                    hwm[ev[0]] = ev[1]


class Stream:

    def __init__(self, S, wsrc, wscr, slots, offsets, plan):
        self.S = S
        self.wsrc = wsrc
        self.wscr = wscr
        self.slots = slots
        self.off = offsets
        self.plan = plan
        self.fetched = 0
        self.consumed = 0
        self.use_ev = [None] * len(plan)
        self.wb_read_ev = [None] * len(plan)
        self.ready = [None] * len(plan)
        self.wb_of_key = {}

    def _fetch_one(self):
        g = self.fetched
        if g >= len(self.plan):
            return
        key = self.plan[g]
        co, ncols = self.off[key]
        slot = self.slots[g % RING]
        war = []
        if g >= RING:
            war.append(self.use_ev[g - RING])
            war.append(self.wb_read_ev[g - RING])
        dst = slot[:, 0:ncols]
        if key not in self.wb_of_key:
            src = self.wsrc[:, co:co + ncols]
            ev = self.S.op("gp", lambda e, s=src, d=dst: e.dma_start(out=d, in_=s),
                           waits=war, inc="wdma", inc_by=16)
            sdst = self.wscr[:, co:co + ncols]
            wb = self.S.op("sp", lambda e, s=dst, d=sdst: e.dma_start(out=d, in_=s),
                           waits=[ev], inc="wb", inc_by=16)
            self.wb_of_key[key] = wb
            self.wb_read_ev[g] = wb
        else:
            src = self.wscr[:, co:co + ncols]
            ev = self.S.op("sp", lambda e, s=src, d=dst: e.dma_start(out=d, in_=s),
                           waits=war + [self.wb_of_key[key]], inc="wb", inc_by=16)
        self.ready[g] = ev
        self.fetched += 1

    def prime(self):
        for _ in range(RING):
            self._fetch_one()

    def consume(self, key):
        g = self.consumed
        assert self.plan[g] == key, (self.plan[g], key)
        return g, self.slots[g % RING], self.ready[g]

    def release(self, g, last_use_ev):
        assert g == self.consumed
        self.use_ev[g] = last_use_ev
        self.consumed += 1
        self._fetch_one()


def build_program(n_chunks=NCH, phases="full", debug=False, LN_UPTO=3):
    offsets, total_cols = _unit_offsets()
    nc = bass.Bass("TRN2", target_bir_lowering=False)

    x_d = nc.dram_tensor("x", [NCH * CT, D], F32, kind="ExternalInput").ap()
    ws_d = nc.dram_tensor("wstream", [P, total_cols], F32, kind="ExternalInput").ap()
    cb_d = nc.dram_tensor("cb", [P, 3 * P], F32, kind="ExternalInput").ap()
    wn_d = nc.dram_tensor("wn", [P, KD + 1], F32, kind="ExternalInput").ap()
    y_d = nc.dram_tensor("y", [NCH * CT, D], F32, kind="ExternalOutput").ap()
    wscr_d = nc.dram_tensor("wscratch", [P, total_cols], BF16, kind="Internal").ap()
    if debug:
        dbg_xnT = nc.dram_tensor("dbg_xnT", [P, KD, CT], BF16, kind="ExternalOutput").ap()
        dbg_cw = nc.dram_tensor("dbg_cw", [P, TT, 8], F32, kind="ExternalOutput").ap()
        dbg_rt = nc.dram_tensor("dbg_rt", [P, TT, 8], F32, kind="ExternalOutput").ap()
        dbg_rs = nc.dram_tensor("dbg_rs", [P, TT, 8], F32, kind="ExternalOutput").ap()

    x_r = x_d.rearrange("(c tt p) h -> c p tt h", c=NCH, tt=TT, p=P)
    y_r = y_d.rearrange("(c tt p) h -> c p tt h", c=NCH, tt=TT, p=P)

    S = Sched()
    es = contextlib.ExitStack()
    with es:
        def sb(name, shape, dt):
            return es.enter_context(nc.sbuf_tensor(name, shape, dt))

        rbuf = [sb("rbuf0", [P, TT * D], F32), sb("rbuf1", [P, TT * D], F32)]
        xnT = sb("xnT", [P, KD, CT], BF16)
        ghat = sb("ghat", [P, FKC, CT], BF16)
        xn_tm = sb("xn_tm", [P, 2, D], BF16)
        sgr = sb("sgr", [P, 2, CT], BF16)
        uprot = sb("uprot", [P, 2, CT], BF16)
        xgy = sb("xgy", [P, KD, CT], BF16)
        a_rot = sb("a_rot", [P, 2, CT], F32)
        ona_rot = sb("ona_rot", [P, 2, CT], F32)
        sgi_rot = sb("sgi_rot", [P, 2, CT], F32)
        si_rot = sb("si_rot", [P, 2, CT], F32)
        b_rot = sb("b_rot", [P, 2, CT], F32)
        h_rot = sb("h_rot", [P, 2, CT], F32)
        sgw_rot = sb("sgw_rot", [P, 2, CT], F32)
        carry = sb("carry", [P, KD], F32)
        r_tmp = sb("r_tmp", [P, CT], F32)
        r_b = sb("r_b", [P, CT], F32)
        rt = sb("rt", [P, TT, 8], F32)
        rex = sb("rex", [P, TT, 8], F32)
        rs = sb("rs", [P, TT, 8], F32)
        rmask = sb("rmask", [P, TT, 8], F32)
        cw = sb("cw", [P, TT, 8], F32)
        rmx = sb("rmx", [P, TT, 1], F32)
        rnmx = sb("rnmx", [P, TT, 1], F32)
        rm1 = sb("rm1", [P, TT, 1], F32)
        rm2 = sb("rm2", [P, TT, 1], F32)
        rz = sb("rz", [P, TT, 1], F32)
        rzi = sb("rzi", [P, TT, 1], F32)
        rden = sb("rden", [P, TT, 1], F32)
        rdi = sb("rdi", [P, TT, 1], F32)
        ln_bnst = sb("ln_bnst", [P, TT, 4, 6], F32)
        ln_mv = sb("ln_mv", [P, TT, 2], F32)
        ln_t = sb("ln_t", [P, TT, 1], F32)
        ln_rstd = sb("ln_rstd", [P, TT, 1], F32)
        ln_nmr = sb("ln_nmr", [P, TT, 1], F32)
        id_sb = sb("id_sb", [P, P], BF16)
        ones_sb = sb("ones_sb", [P, P], BF16)
        gw_sb = sb("gw_sb", [P, P], BF16)
        wn_sb = sb("wn_sb", [P, KD + 1], F32)
        slots = [sb(f"wslot{i}", [P, SLOT_COLS], BF16) for i in range(RING)]

        ps = [es.enter_context(nc.psum_tensor(f"ps{i}", [P, CT], F32)) for i in range(6)]
        ps_trb = [es.enter_context(nc.psum_tensor(f"ps_tr{i}", [P, 8, P], BF16))
                  for i in range(2)]

        sem_names = ["xin", "xout", "wdma", "wb", "const", "pe", "act", "dve"]
        sems = {n: es.enter_context(nc.semaphore("s_" + n)) for n in sem_names}

        stream = Stream(S, ws_d, wscr_d, slots, offsets, _plan_units(n_chunks, phases))

        S.op("gp", lambda e: e.dma_start(out=id_sb[:], in_=cb_d[:, 0:P]),
             inc="const", inc_by=16)
        S.op("gp", lambda e: e.dma_start(out=ones_sb[:], in_=cb_d[:, P:2 * P]),
             inc="const", inc_by=16)
        S.op("gp", lambda e: e.dma_start(out=gw_sb[:], in_=cb_d[:, 2 * P:3 * P]),
             inc="const", inc_by=16)
        S.op("sp", lambda e: e.dma_start(out=wn_sb[:], in_=wn_d),
             inc="const", inc_by=16)
        stream.prime()
        const_ev = ("const", S.cnt["const"])

        st = {
            "resid_dve": None,
            "xnT_evts": [],
            "sq_act": None,
            "tm_tr": [None, None],
                        "bank": [None] * 6,
            "tr_evac": [None, None],
            "cw_ev": None,
            "tr_bnk": 0,
            "ln_apply": {},
            "xnT_tt": [None] * TT,
            "xgy_pe": None,
            "sgr_dve": [None, None],
            "up_dve": [None, None],
            "ghat_pe": None,
            "a_read": [None, None], "ona_read": [None, None],
            "sgi_read": [None, None], "sgw_read": [None, None],
            "h_read": [None, None], "b_read": [None, None], "si_read": [None, None],
        }

        def emit_ln_q(c, site, tt, q, RS):
            deps = []
            if site == 1:
                deps.append(("xin", 16 * (c + 1)))
            S.op("dve", lambda e, tt=tt, q=q, RS=RS: e.bn_stats(
                ln_bnst[:, tt, q, :], RS[:, tt, q * 512:(q + 1) * 512]),
                waits=deps, inc="dve")

        def emit_ln_fin(c, site, tt, RS):
            S.op("dve", lambda e, tt=tt: e.bn_aggr(
                ln_mv[:, tt, :], ln_bnst[:, tt].rearrange("p a b -> p (a b)")),
                inc="dve")
            ev_d = ("dve", S.cnt["dve"])
            S.op("act", lambda e, tt=tt: e.activation(
                ln_t[:, tt, :], ln_mv[:, tt, 1:2], AF.Ln, bias=wn_sb[:, KD:KD + 1]),
                waits=[ev_d], inc="act")
            ev_r = S.op("act", lambda e, tt=tt: e.activation(
                ln_rstd[:, tt, :], ln_t[:, tt, :], AF.Exp, scale=-0.5), inc="act")
            ev_nmr = S.op("dve", lambda e, tt=tt: e.tensor_scalar(
                ln_nmr[:, tt, :], ln_mv[:, tt, 0:1], ln_rstd[:, tt, :], -1.0,
                OP.mult, OP.mult),
                waits=[ev_r], inc="dve")
            st["ln_apply"][(site, tt)] = ev_nmr

        def emit_ln_stats_tt(c, site, tt, RS):
            for q in range(4):
                emit_ln_q(c, site, tt, q, RS)
            emit_ln_fin(c, site, tt, RS)

        def emit_ln(c, site, upto=3):
            RS = st["RS"]
            evts = []
            for tt in range(TT):
                if (site, tt) not in st["ln_apply"]:
                    emit_ln_stats_tt(c, site, tt, RS)
                ev_nmr = st["ln_apply"].pop((site, tt))
                par = tt % 2
                ev_ap_h = [None, None]
                ev_ap_h[0] = S.op("act", lambda e, tt=tt, par=par, RS=RS: e.activation(
                    xn_tm[:, par, 0:D // 2], RS[:, tt, 0:D // 2], AF.Identity,
                    bias=ln_nmr[:, tt, :], scale=ln_rstd[:, tt, :]),
                    waits=[ev_nmr, st["tm_tr"][par], st["resid_dve"]], inc="act")
                ev_ap_h[1] = S.op("act", lambda e, tt=tt, par=par, RS=RS: e.activation(
                    xn_tm[:, par, D // 2:D], RS[:, tt, D // 2:D], AF.Identity,
                    bias=ln_nmr[:, tt, :], scale=ln_rstd[:, tt, :]),
                    waits=[ev_nmr], inc="act")
                for batch in range(KD // 8):
                    bnk = st["tr_bnk"]
                    st["tr_bnk"] = 1 - bnk
                    for j in range(8):
                        k = batch * 8 + j
                        w0 = [ev_ap_h[batch], const_ev]
                        if j == 0:
                            w0.append(st["tr_evac"][bnk])
                        ev_tr = S.op("pe", lambda e, k=k, j=j, bnk=bnk, par=par:
                                     e.transpose(ps_trb[bnk][:, j, :],
                                                 xn_tm[:, par, k * P:(k + 1) * P],
                                                 id_sb[:]),
                                     waits=w0, inc="pe" if j == 7 else None)
                    st["tm_tr"][par] = ev_tr
                    k0 = batch * 8
                    ev_ev = S.op("dve", lambda e, tt=tt, k0=k0, bnk=bnk:
                                 e.tensor_copy(xnT[:, k0:k0 + 8, tt * P:(tt + 1) * P],
                                               ps_trb[bnk][:]),
                                 waits=[ev_tr], inc="dve")
                    st["tr_evac"][bnk] = ev_ev
                    evts.append(ev_ev)
                st["xnT_tt"][tt] = ev_ev
            st["xnT_evts"] = [evts[-1]]

        def emit_router(c, call):
            for tt in range(TT):
                bk = 2 + tt
                for k in range(KD):
                    S.op("pe", lambda e, tt=tt, k=k, bk=bk: e.matmul(
                        ps[bk][:, 0:8], xnT[:, k, tt * P:(tt + 1) * P],
                        gw_sb[:, k * 8:(k + 1) * 8],
                        start=(k == 0), stop=(k == KD - 1),
                        skip_group_check=True),
                        waits=([st["xnT_tt"][tt], st["bank"][bk], const_ev]) if k == 0 else [],
                        inc="pe" if k == KD - 1 else None)
                ev_mm = ("pe", S.cnt["pe"])
                ev_cp = S.op("dve", lambda e, tt=tt, bk=bk: e.tensor_copy(rt[:, tt, :], ps[bk][:, 0:8]),
                             waits=[ev_mm], inc="dve")
                st["bank"][bk] = ev_cp
            ev_r = S.op("dve", lambda e: e.reduce_max(rmx[:], rt[:], axis=AX.X), inc="dve")
            S.selfwait("dve", ev_r)
            S.op("dve", lambda e: e.tensor_single_scalar(rnmx[:], rmx[:], -1.0, OP.mult),
                 inc="dve")
            ev_n = ("dve", S.cnt["dve"])
            exev = []
            for tt in range(TT):
                exev.append(S.op("act", lambda e, tt=tt: e.activation(
                    rex[:, tt, :], rt[:, tt, :], AF.Exp, bias=rnmx[:, tt, :]),
                    waits=[ev_n], inc="act"))
            ev_r = S.op("dve", lambda e: e.reduce_sum(rz[:], rex[:], axis=AX.X),
                 waits=[exev[-1]], inc="dve")
            S.selfwait("dve", ev_r)
            S.op("dve", lambda e: e.reciprocal(rzi[:], rz[:]), inc="dve")
            for tt in range(TT):
                S.op("dve", lambda e, tt=tt: e.tensor_single_scalar(
                    rs[:, tt, :], rex[:, tt, :], rzi[:, tt, :], OP.mult), inc="dve")
            ev_r = S.op("dve", lambda e: e.reduce_max(rm1[:], rs[:], axis=AX.X), inc="dve")
            S.selfwait("dve", ev_r)
            for tt in range(TT):
                S.op("dve", lambda e, tt=tt: e.tensor_single_scalar(
                    rmask[:, tt, :], rs[:, tt, :], rm1[:, tt, :], OP.is_ge), inc="dve")
                S.op("dve", lambda e, tt=tt: e.scalar_tensor_tensor(
                    rex[:, tt, :], rmask[:, tt, :], -1e30, rs[:, tt, :],
                    OP.mult, OP.add), inc="dve")
            ev_r = S.op("dve", lambda e: e.reduce_max(rm2[:], rex[:], axis=AX.X), inc="dve")
            S.selfwait("dve", ev_r)
            for tt in range(TT):
                S.op("dve", lambda e, tt=tt: e.tensor_single_scalar(
                    rmask[:, tt, :], rs[:, tt, :], rm2[:, tt, :], OP.is_ge), inc="dve")
            S.op("dve", lambda e: e.tensor_add(rden[:], rm1[:], rm2[:]), inc="dve")
            S.op("dve", lambda e: e.tensor_single_scalar(rden[:], rden[:], 1e-20, OP.add),
                 inc="dve")
            S.op("dve", lambda e: e.reciprocal(rdi[:], rden[:]), inc="dve")
            S.op("dve", lambda e: e.tensor_single_scalar(rdi[:], rdi[:], 0.5, OP.mult),
                 inc="dve")
            for tt in range(TT):
                S.op("dve", lambda e, tt=tt: e.tensor_mul(
                    rex[:, tt, :], rs[:, tt, :], rmask[:, tt, :]), inc="dve")
                ev = S.op("dve", lambda e, tt=tt: e.tensor_single_scalar(
                    cw[:, tt, :], rex[:, tt, :], rdi[:, tt, :], OP.mult), inc="dve")
            st["cw_ev"] = ev

        def emit_expert(c, call, u, tail_hook=None):
            gh_evts = []
            slot_info = None
            ev_g = {}
            ev_up_copy = {-1: None}
            for fc in range(FKC):
                j = fc // 2
                fl = fc % 2
                if fl == 0:
                    g_id, slot, ready = stream.consume(("gu", u, j))
                    slot_info = (g_id, slot, ready)
                g_id, slot, ready = slot_info
                gb = fc % 2

                def gtile(slot, fl, gu, k):
                    base = (fl * 2 + gu) * 2048 + k * P
                    return slot[:, base:base + P]

                for k in range(KD):
                    w0 = []
                    if k == 0:
                        w0 = st["xnT_evts"] + [ready, st["bank"][0]]
                    S.op("pe", lambda e, slot=slot, fl=fl, k=k: e.matmul(
                        ps[0][:], gtile(slot, fl, 0, k), xnT[:, k, :],
                        start=(k == 0), stop=(k == KD - 1)),
                        waits=w0, inc="pe" if k == KD - 1 else None)
                ev_gate = ("pe", S.cnt["pe"])
                for k in range(KD):
                    w0 = [ev_up_copy[fc - 1], st["bank"][1]] if k == 0 else []
                    S.op("pe", lambda e, slot=slot, fl=fl, k=k: e.matmul(
                        ps[1][:], gtile(slot, fl, 1, k), xnT[:, k, :],
                        start=(k == 0), stop=(k == KD - 1)),
                        waits=w0, inc="pe" if k == KD - 1 else None)
                ev_up = ("pe", S.cnt["pe"])
                if fc == FKC - 1:
                    stream.release(g_id, ev_up)
                elif fl == 1:
                    stream.release(g_id, ev_up)
                ev_si = S.op("act", lambda e, gb=gb: e.activation(
                    sgr[:, gb, :], ps[0][:], AF.Silu),
                    waits=[ev_gate, st["sgr_dve"][gb]], inc="act")
                st["bank"][0] = ev_si
                ev_uc = S.op("act", lambda e, gb=gb: e.copy(uprot[:, gb, :], ps[1][:]),
                             waits=[ev_up, st["up_dve"][gb]], inc="act")
                ev_up_copy[fc] = ev_uc
                st["bank"][1] = ev_uc
                ev_gh = S.op("dve", lambda e, fc=fc, gb=gb: e.tensor_mul(
                    ghat[:, fc, :], sgr[:, gb, :], uprot[:, gb, :]),
                    waits=[ev_si, ev_uc, st["ghat_pe"]], inc="dve")
                st["sgr_dve"][gb] = ev_gh
                st["up_dve"][gb] = ev_gh
                gh_evts.append(ev_gh)
            ghat_ready = gh_evts[-1]

            RS = st["RS"]
            for hq in range(HQN):
                g_id, slot, ready = stream.consume(("wd", u, hq))
                last_grp = None
                for tt in range(TT):
                    bk = 2 + tt
                    for f in range(FKC):
                        w0 = [gh_evts[f]]
                        if f == 0:
                            w0 += [ready, st["bank"][bk]]
                        S.op("pe", lambda e, slot=slot, f=f, tt=tt, bk=bk: e.matmul(
                            ps[bk][:], ghat[:, f, tt * P:(tt + 1) * P],
                            slot[:, f * 512:(f + 1) * 512],
                            start=(f == 0), stop=(f == FKC - 1)),
                            waits=w0, inc="pe" if f == FKC - 1 else None)
                    ev_grp = ("pe", S.cnt["pe"])
                    last_grp = ev_grp
                    if u < 2:
                        cwsc = 0.5
                    else:
                        cwsc = cw[:, tt, (u - 2):(u - 1)]
                    ev_ev = S.op("dve", lambda e, tt=tt, hq=hq, bk=bk, cwsc=cwsc, RS=RS:
                                 e.scalar_tensor_tensor(
                                     RS[:, tt, hq * 512:(hq + 1) * 512], ps[bk][:],
                                     cwsc, RS[:, tt, hq * 512:(hq + 1) * 512],
                                     OP.mult, OP.add),
                                 waits=[ev_grp, st["cw_ev"]], inc="dve")
                    st["bank"][bk] = ev_ev
                    st["resid_dve"] = ev_ev
                    if tail_hook is not None:
                        tail_hook(tt, hq)
                stream.release(g_id, last_grp)
            st["ghat_pe"] = last_grp

        def emit_hgrn(c, tail_hook=None):
            RS, XG = st["RS"], st["XG"]
            sq_evts = []
            for et in range(KD):
                par = et % 2
                g_id, slot, ready = stream.consume(("ifg", et, 0))
                pe_ev = {}
                for mi, bk in ((0, 0), (1, 1), (2, 2)):
                    for k in range(KD):
                        w0 = []
                        if k == 0:
                            w0 = st["xnT_evts"] + [ready, st["bank"][bk]]
                        S.op("pe", lambda e, slot=slot, mi=mi, k=k, bk=bk: e.matmul(
                            ps[bk][:], slot[:, (mi * KD + k) * P:(mi * KD + k + 1) * P],
                            xnT[:, k, :],
                            start=(k == 0), stop=(k == KD - 1)),
                            waits=w0, inc="pe" if k == KD - 1 else None)
                    pe_ev[mi] = ("pe", S.cnt["pe"])
                stream.release(g_id, pe_ev[2])
                ev_a = S.op("act", lambda e, par=par: e.activation(
                    a_rot[:, par, :], ps[0][:], AF.Sigmoid),
                    waits=[pe_ev[0], st["a_read"][par]], inc="act")
                ev_ona = S.op("act", lambda e, par=par: e.activation(
                    ona_rot[:, par, :], ps[0][:], AF.Sigmoid, scale=-1.0),
                    waits=[st["ona_read"][par]], inc="act")
                st["bank"][0] = ev_ona
                ev_sgi = S.op("act", lambda e, par=par: e.activation(
                    sgi_rot[:, par, :], ps[1][:], AF.Sigmoid),
                    waits=[pe_ev[1], st["sgi_read"][par]], inc="act")
                ev_sgg = S.op("act", lambda e, par=par: e.activation(
                    sgw_rot[:, par, :], ps[2][:], AF.Sigmoid),
                    waits=[pe_ev[2], st["sgw_read"][par]], inc="act")
                ev_si = S.op("dve", lambda e, par=par: e.tensor_mul(
                    si_rot[:, par, :], sgi_rot[:, par, :], ps[1][:]),
                    waits=[ev_sgi, st["si_read"][par]], inc="dve")
                st["sgi_read"][par] = ev_si
                st["bank"][1] = ev_si
                ev_b = S.op("dve", lambda e, par=par: e.tensor_mul(
                    b_rot[:, par, :], si_rot[:, par, :], ona_rot[:, par, :]),
                    waits=[ev_ona, st["b_read"][par]], inc="dve")
                st["si_read"][par] = ev_b
                st["ona_read"][par] = ev_b
                ev_sgw = S.op("dve", lambda e, par=par, et=et: e.scalar_tensor_tensor(
                    sgw_rot[:, par, :], sgw_rot[:, par, :], wn_sb[:, et:et + 1],
                    ps[2][:], OP.mult, OP.mult),
                    waits=[ev_sgg], inc="dve")
                st["bank"][2] = ev_sgw
                init = 0.0 if c == 0 else carry[:, et:et + 1]
                ev_h = S.op("dve", lambda e, par=par, init=init: e.tensor_tensor_scan(
                    h_rot[:, par, :], a_rot[:, par, :], b_rot[:, par, :], init,
                    OP.mult, OP.add),
                    waits=[ev_a, st["h_read"][par]], inc="dve")
                st["a_read"][par] = ev_h
                st["b_read"][par] = ev_h
                S.op("dve", lambda e, et=et, par=par: e.tensor_copy(
                    carry[:, et:et + 1], h_rot[:, par, CT - 1:CT]), inc="dve")
                ev_xg = S.op("dve", lambda e, et=et, par=par, XG=XG: e.tensor_mul(
                    XG[:, et, :], h_rot[:, par, :], sgw_rot[:, par, :]),
                    waits=[st["sq_act"], ("xout", 16 * c)], inc="dve")
                st["h_read"][par] = ev_xg
                st["sgw_read"][par] = ev_xg
                ev_sq = S.op("act", lambda e, et=et, XG=XG: e.activation(
                    xgy[:, et, :], XG[:, et, :], AF.Square),
                    waits=[ev_xg, st["xgy_pe"]], inc="act")
                sq_evts.append(ev_sq)
            st["sq_act"] = sq_evts[-1]

            for et in range(KD):
                S.op("pe", lambda e, et=et: e.matmul(
                    ps[2][:], ones_sb[:], xgy[:, et, :],
                    start=(et == 0), stop=(et == KD - 1)),
                    waits=[sq_evts[et], st["bank"][2]] if et == 0 else [sq_evts[et]],
                    inc="pe" if et == KD - 1 else None)
            ev_ones = ("pe", S.cnt["pe"])
            S.op("dve", lambda e: e.tensor_single_scalar(r_tmp[:], ps[2][:], 1.0 / D, OP.mult),
                 waits=[ev_ones], inc="dve")
            ev_rt = ("dve", S.cnt["dve"])
            st["bank"][2] = ev_rt
            S.op("act", lambda e: e.activation(r_tmp[:], r_tmp[:], AF.Ln, bias=wn_sb[:, KD:KD + 1]),
                 waits=[ev_rt], inc="act")
            ev_rb = S.op("act", lambda e: e.activation(r_b[:], r_tmp[:], AF.Exp, scale=-0.5),
                         inc="act")
            y_evts = []
            for et in range(KD):
                y_evts.append(S.op("dve", lambda e, et=et, XG=XG: e.tensor_mul(
                    xgy[:, et, :], XG[:, et, :], r_b[:]),
                    waits=[ev_rb, ev_ones], inc="dve"))
            st["xload_deps"] = [y_evts[-1], sq_evts[-1]]
            last_grp = None
            for dq in range(4):
                g_id, slot, ready = stream.consume(("wo", dq, 0))
                for et in range(KD):
                    for tt in range(TT):
                        bk = 2 + tt
                        w0 = [y_evts[et]]
                        if et == 0:
                            w0 = [ready, y_evts[et], st["bank"][bk]]
                        S.op("pe", lambda e, slot=slot, et=et, tt=tt, bk=bk: e.matmul(
                            ps[bk][:], xgy[:, et, tt * P:(tt + 1) * P],
                            slot[:, et * 512:(et + 1) * 512],
                            start=(et == 0), stop=(et == KD - 1),
                            skip_group_check=True),
                            waits=w0, inc="pe" if (et == KD - 1) else None)
                        if et == KD - 1:
                            ev_grp = ("pe", S.cnt["pe"])
                            ev_ev = S.op("dve", lambda e, tt=tt, dq=dq, bk=bk, RS=RS:
                                         e.tensor_tensor(
                                             RS[:, tt, dq * 512:(dq + 1) * 512],
                                             RS[:, tt, dq * 512:(dq + 1) * 512],
                                             ps[bk][:], OP.add),
                                         waits=[ev_grp], inc="dve")
                            st["bank"][bk] = ev_ev
                            st["resid_dve"] = ev_ev
                            last_grp = ev_grp
                            if tail_hook is not None:
                                tail_hook(tt, dq)
                stream.release(g_id, last_grp)
            st["xgy_pe"] = last_grp

        st["xload_deps"] = []
        for c in range(n_chunks):
            RS = rbuf[c % 2][:].rearrange("p (a b) -> p a b", a=TT)
            XG = rbuf[1 - (c % 2)][:].rearrange("p (a b) -> p a b", a=KD)
            st["RS"], st["XG"] = RS, XG
            if c == 0 or phases != "full":
                w_load = ([("xout", 16 * c)] if c > 0 else []) + st["xload_deps"]
                S.op("sp", lambda e, c=c, RS=RS: e.dma_start(out=RS, in_=x_r[c]),
                     waits=w_load, inc="xin", inc_by=16)
            if phases == "copy":
                S.op("sp", lambda e, c=c, RS=RS: e.dma_start(out=y_r[c], in_=RS),
                     waits=[("xin", 16 * (c + 1))], inc="xout", inc_by=16)
                continue
            emit_ln(c, 1, upto=LN_UPTO)
            if debug and c == 0 and phases in ("ln1", "ffn1"):
                S.op("sp", lambda e: e.dma_start(out=dbg_xnT, in_=xnT[:]),
                     waits=st["xnT_evts"], inc="xout", inc_by=16)
            if phases in ("ffn1", "attn", "full"):
                emit_router(c, 0)
                if debug and c == 0:
                    S.op("sp", lambda e: e.dma_start(out=dbg_cw, in_=cw[:]),
                         waits=[st["cw_ev"]], inc="xout", inc_by=16)
                def hook2(tt, q, c=c, RS=RS):
                    emit_ln_q(c, 2, tt, q, RS)
                    if q == 3:
                        emit_ln_fin(c, 2, tt, RS)
                if phases not in ("attn", "full"):
                    hook2 = None
                for u in range(NU):
                    emit_expert(c, 0, u, tail_hook=hook2 if u == NU - 1 else None)
            if phases in ("attn", "full"):
                emit_ln(c, 2)
                def hook3(tt, q, c=c, RS=RS):
                    emit_ln_q(c, 3, tt, q, RS)
                    if q == 3:
                        emit_ln_fin(c, 3, tt, RS)
                if phases != "full":
                    hook3 = None
                emit_hgrn(c, tail_hook=hook3)
                if phases == "full" and c + 1 < n_chunks:
                    nxt = rbuf[(c + 1) % 2][:].rearrange("p (a b) -> p a b", a=TT)
                    S.op("sp", lambda e, c=c, nxt=nxt: e.dma_start(
                        out=nxt, in_=x_r[c + 1]),
                        waits=st["xload_deps"], inc="xin", inc_by=16)
            if phases == "full":
                emit_ln(c, 3)
                emit_router(c, 1)
                if c + 1 < n_chunks:
                    RSn = rbuf[(c + 1) % 2][:].rearrange("p (a b) -> p a b", a=TT)
                    def hook1(tt, q, c=c, RSn=RSn):
                        emit_ln_q(c + 1, 1, tt, q, RSn)
                        if q == 3:
                            emit_ln_fin(c + 1, 1, tt, RSn)
                else:
                    hook1 = None
                for u in range(NU):
                    emit_expert(c, 1, u, tail_hook=hook1 if u == NU - 1 else None)
            S.op("sp", lambda e, c=c, RS=RS: e.dma_start(out=y_r[c], in_=RS),
                 waits=[st["resid_dve"], ("dve", S.cnt["dve"])], inc="xout", inc_by=16)

        final_out = ("xout", S.cnt["xout"])
        S.op("sp", None, waits=[final_out])

        with nc.Block() as block:
            @block.sync
            def _(eng):
                S.replay("sp", eng, sems)

            @block.gpsimd
            def _(eng):
                S.replay("gp", eng, sems)

            @block.tensor
            def _(eng):
                S.replay("pe", eng, sems)

            @block.scalar
            def _(eng):
                S.replay("act", eng, sems)

            @block.vector
            def _(eng):
                S.replay("dve", eng, sems)

    return nc


def kernel(L0_attn_x, L0_attn_Wi, L0_attn_Wf, L0_attn_Wg, L0_attn_w_norm,
           L0_attn_Wo, L0_ffn_gate_weight, L0_ffn_w_gate, L0_ffn_w_up,
           L0_ffn_w_down, L0_ffn_sw_gate, L0_ffn_sw_up, L0_ffn_sw_down):
    inp = dict(L0_attn_Wi=L0_attn_Wi, L0_attn_Wf=L0_attn_Wf, L0_attn_Wg=L0_attn_Wg,
               L0_attn_w_norm=L0_attn_w_norm, L0_attn_Wo=L0_attn_Wo,
               L0_ffn_gate_weight=L0_ffn_gate_weight, L0_ffn_w_gate=L0_ffn_w_gate,
               L0_ffn_w_up=L0_ffn_w_up, L0_ffn_w_down=L0_ffn_w_down,
               L0_ffn_sw_gate=L0_ffn_sw_gate, L0_ffn_sw_up=L0_ffn_sw_up,
               L0_ffn_sw_down=L0_ffn_sw_down)
    wstream, cb, wn = _pack_host(inp)
    x = np.asarray(L0_attn_x, dtype=np.float32)
    nc = build_program()
    in_maps = [dict(x=np.ascontiguousarray(x[b]), wstream=wstream, cb=cb, wn=wn)
               for b in range(8)]
    res = run_bass_kernel_spmd(nc, in_maps, core_ids=list(range(8)))
    return np.stack([r["y"] for r in res.results], axis=0)


# version 10
# speedup vs baseline: 1.1883x; 1.1732x over previous
import contextlib
import numpy as np
import ml_dtypes
import concourse.bass as bass
import concourse.mybir as mybir
from concourse.bass_utils import run_bass_kernel_spmd

F32 = mybir.dt.float32
BF16 = mybir.dt.bfloat16
AF = mybir.ActivationFunctionType
OP = mybir.AluOpType
AX = mybir.AxisListType

P = 128
D = 2048
KD = 16
CT = 512
TT = 4
NCH = 8
FKC = 11
NU = 10
NGU = 6
HQN = 4
EPS = 1e-5
RING = 3
SLOT_COLS = 8192

GU_UNIT_COLS = [4096 * (2 if j < 5 else 1) for j in range(NGU)]
WD_UNIT_COLS = FKC * 512
IFG_UNIT_COLS = 3 * KD * P
WO_UNIT_COLS = KD * 512


def _unit_offsets():
    off = {}
    c = 0
    for u in range(NU):
        for j in range(NGU):
            off[("gu", u, j)] = (c, GU_UNIT_COLS[j])
            c += GU_UNIT_COLS[j]
        for hq in range(HQN):
            off[("wd", u, hq)] = (c, WD_UNIT_COLS)
            c += WD_UNIT_COLS
    for et in range(KD):
        off[("ifg", et, 0)] = (c, IFG_UNIT_COLS)
        c += IFG_UNIT_COLS
    for dq in range(4):
        off[("wo", dq, 0)] = (c, WO_UNIT_COLS)
        c += WO_UNIT_COLS
    return off, c


def _plan_units(n_chunks, phases):
    plan = []

    def ffn():
        for u in range(NU):
            for j in range(NGU):
                plan.append(("gu", u, j))
            for hq in range(HQN):
                plan.append(("wd", u, hq))

    for _c in range(n_chunks):
        if phases in ("ffn1", "attn", "full"):
            ffn()
        if phases in ("attn", "full"):
            for et in range(KD):
                plan.append(("ifg", et, 0))
            for dq in range(4):
                plan.append(("wo", dq, 0))
        if phases == "full":
            ffn()
    return plan


def _pack_host(inp):
    def t(a):
        return np.ascontiguousarray(np.asarray(a, dtype=np.float32).T)

    blocks = []
    swg = t(inp["L0_ffn_sw_gate"])
    swu = t(inp["L0_ffn_sw_up"])
    swd = t(inp["L0_ffn_sw_down"])
    mats = [
        (swg[:, :1408], swu[:, :1408], swd[:1408, :]),
        (swg[:, 1408:], swu[:, 1408:], swd[1408:, :]),
    ]
    wg = np.asarray(inp["L0_ffn_w_gate"], dtype=np.float32)
    wu = np.asarray(inp["L0_ffn_w_up"], dtype=np.float32)
    wd = np.asarray(inp["L0_ffn_w_down"], dtype=np.float32)
    for e in range(8):
        mats.append((t(wg[e]), t(wu[e]), t(wd[e])))
    for (g, u_, d_) in mats:
        ag = g.reshape(KD, P, FKC, P).transpose(2, 0, 1, 3)
        au = u_.reshape(KD, P, FKC, P).transpose(2, 0, 1, 3)
        gu = np.stack([ag, au], axis=1)
        blocks.append(gu.transpose(3, 0, 1, 2, 4).reshape(P, -1))
        ad = d_.reshape(FKC, P, HQN, 512).transpose(2, 0, 1, 3)
        blocks.append(ad.transpose(2, 0, 1, 3).reshape(P, -1))

    def proj_arr(wT):
        return wT.reshape(KD, P, KD, P).transpose(2, 0, 1, 3)

    af_ = proj_arr(t(inp["L0_attn_Wf"]))
    ai_ = proj_arr(t(inp["L0_attn_Wi"]))
    ag_ = proj_arr(t(inp["L0_attn_Wg"]))
    ifg = np.stack([af_, ai_, ag_], axis=1)
    blocks.append(ifg.transpose(3, 0, 1, 2, 4).reshape(P, -1))
    woT = t(inp["L0_attn_Wo"])
    ao = woT.reshape(KD, P, 4, 512).transpose(2, 0, 1, 3)
    blocks.append(ao.transpose(2, 0, 1, 3).reshape(P, -1))

    wstream = np.ascontiguousarray(np.concatenate(blocks, axis=1), dtype=np.float32)

    gwT = t(inp["L0_ffn_gate_weight"])
    gw_arr = gwT.reshape(KD, P, 8).transpose(1, 0, 2).reshape(P, KD * 8)
    cb = np.ascontiguousarray(
        np.concatenate([np.eye(P, dtype=np.float32),
                        np.ones((P, P), dtype=np.float32),
                        gw_arr], axis=1))
    wn = np.asarray(inp["L0_attn_w_norm"], dtype=np.float32).reshape(KD, P).T
    wn = np.ascontiguousarray(
        np.concatenate([wn, np.full((P, 1), EPS, dtype=np.float32)], axis=1))
    return wstream, cb, wn


class Sched:
    OWNER = {"pe": "pe", "act": "act", "dve": "dve"}

    def __init__(self):
        self.q = {k: [] for k in ("pe", "act", "dve", "gp", "sp")}
        self.cnt = {}
        self.n = 0

    def op(self, eng, fn, waits=(), inc=None, inc_by=1):
        ev = None
        if inc is not None:
            self.cnt[inc] = self.cnt.get(inc, 0) + inc_by
            ev = (inc, self.cnt[inc])
        wl = [w for w in waits if w is not None and self.OWNER.get(w[0]) != eng]
        self.q[eng].append((wl, fn, ev, inc_by))
        self.n += 1
        return ev

    def selfwait(self, eng, ev):
        self.q[eng].append(([ev], None, None, 1))

    AUTO_SELFWAIT = ("dve", "act")

    def replay(self, name, eng, sems):
        hwm = {}
        auto = name in self.AUTO_SELFWAIT
        for wl, fn, ev, inc_by in self.q[name]:
            for (s, v) in wl:
                if hwm.get(s, 0) < v:
                    eng.wait_ge(sems[s], v)
                    hwm[s] = v
            if fn is None:
                continue
            ins = fn(eng)
            if ev is not None:
                ins.then_inc(sems[ev[0]], inc_by)
                if auto:
                    eng.wait_ge(sems[ev[0]], ev[1])
                    hwm[ev[0]] = ev[1]


class Stream:

    def __init__(self, S, wsrc, wscr, slots, offsets, plan):
        self.S = S
        self.wsrc = wsrc
        self.wscr = wscr
        self.slots = slots
        self.off = offsets
        self.plan = plan
        self.fetched = 0
        self.consumed = 0
        self.use_ev = [None] * len(plan)
        self.wb_read_ev = [None] * len(plan)
        self.ready = [None] * len(plan)
        self.wb_of_key = {}

    def _fetch_one(self):
        g = self.fetched
        if g >= len(self.plan):
            return
        key = self.plan[g]
        co, ncols = self.off[key]
        slot = self.slots[g % RING]
        war = []
        if g >= RING:
            war.append(self.use_ev[g - RING])
            war.append(self.wb_read_ev[g - RING])
        dst = slot[:, 0:ncols]
        if key not in self.wb_of_key:
            src = self.wsrc[:, co:co + ncols]
            ev = self.S.op("gp", lambda e, s=src, d=dst: e.dma_start(out=d, in_=s),
                           waits=war, inc="wdma", inc_by=16)
            sdst = self.wscr[:, co:co + ncols]
            wb = self.S.op("sp", lambda e, s=dst, d=sdst: e.dma_start(out=d, in_=s),
                           waits=[ev], inc="wb", inc_by=16)
            self.wb_of_key[key] = wb
            self.wb_read_ev[g] = wb
        else:
            src = self.wscr[:, co:co + ncols]
            q, sem = ("gp", "wdma") if (g % 2 == 0) else ("sp", "wb")
            ev = self.S.op(q, lambda e, s=src, d=dst: e.dma_start(out=d, in_=s),
                           waits=war + [self.wb_of_key[key]], inc=sem, inc_by=16)
        self.ready[g] = ev
        self.fetched += 1

    def prime(self):
        for _ in range(RING):
            self._fetch_one()

    def consume(self, key):
        g = self.consumed
        assert self.plan[g] == key, (self.plan[g], key)
        return g, self.slots[g % RING], self.ready[g]

    def release(self, g, last_use_ev):
        assert g == self.consumed
        self.use_ev[g] = last_use_ev
        self.consumed += 1
        self._fetch_one()


def build_program(n_chunks=NCH, phases="full", debug=False, LN_UPTO=3):
    offsets, total_cols = _unit_offsets()
    nc = bass.Bass("TRN2", target_bir_lowering=False)

    x_d = nc.dram_tensor("x", [NCH * CT, D], F32, kind="ExternalInput").ap()
    ws_d = nc.dram_tensor("wstream", [P, total_cols], F32, kind="ExternalInput").ap()
    cb_d = nc.dram_tensor("cb", [P, 3 * P], F32, kind="ExternalInput").ap()
    wn_d = nc.dram_tensor("wn", [P, KD + 1], F32, kind="ExternalInput").ap()
    y_d = nc.dram_tensor("y", [NCH * CT, D], F32, kind="ExternalOutput").ap()
    wscr_d = nc.dram_tensor("wscratch", [P, total_cols], BF16, kind="Internal").ap()
    if debug:
        dbg_xnT = nc.dram_tensor("dbg_xnT", [P, KD, CT], BF16, kind="ExternalOutput").ap()
        dbg_cw = nc.dram_tensor("dbg_cw", [P, TT, 8], F32, kind="ExternalOutput").ap()
        dbg_rt = nc.dram_tensor("dbg_rt", [P, TT, 8], F32, kind="ExternalOutput").ap()
        dbg_rs = nc.dram_tensor("dbg_rs", [P, TT, 8], F32, kind="ExternalOutput").ap()

    x_r = x_d.rearrange("(c tt p) h -> c p tt h", c=NCH, tt=TT, p=P)
    y_r = y_d.rearrange("(c tt p) h -> c p tt h", c=NCH, tt=TT, p=P)

    S = Sched()
    es = contextlib.ExitStack()
    with es:
        def sb(name, shape, dt):
            return es.enter_context(nc.sbuf_tensor(name, shape, dt))

        rbuf = [sb("rbuf0", [P, TT * D], F32), sb("rbuf1", [P, TT * D], F32)]
        xnT = sb("xnT", [P, KD, CT], BF16)
        ghat = sb("ghat", [P, FKC, CT], BF16)
        xn_tm = sb("xn_tm", [P, 2, D], BF16)
        sgr = sb("sgr", [P, 2, CT], BF16)
        uprot = sb("uprot", [P, 2, CT], BF16)
        xgy = sb("xgy", [P, KD, CT], BF16)
        a_rot = sb("a_rot", [P, 2, CT], F32)
        ona_rot = sb("ona_rot", [P, 2, CT], F32)
        sgi_rot = sb("sgi_rot", [P, 2, CT], F32)
        si_rot = sb("si_rot", [P, 2, CT], F32)
        b_rot = sb("b_rot", [P, 2, CT], F32)
        h_rot = sb("h_rot", [P, 2, CT], F32)
        sgw_rot = sb("sgw_rot", [P, 2, CT], F32)
        carry = sb("carry", [P, KD], F32)
        r_tmp = sb("r_tmp", [P, CT], F32)
        r_b = sb("r_b", [P, CT], F32)
        rt = sb("rt", [P, TT, 8], F32)
        rex = sb("rex", [P, TT, 8], F32)
        rs = sb("rs", [P, TT, 8], F32)
        rmask = sb("rmask", [P, TT, 8], F32)
        cw = sb("cw", [P, TT, 8], F32)
        rmx = sb("rmx", [P, TT, 1], F32)
        rnmx = sb("rnmx", [P, TT, 1], F32)
        rm1 = sb("rm1", [P, TT, 1], F32)
        rm2 = sb("rm2", [P, TT, 1], F32)
        rz = sb("rz", [P, TT, 1], F32)
        rzi = sb("rzi", [P, TT, 1], F32)
        rden = sb("rden", [P, TT, 1], F32)
        rdi = sb("rdi", [P, TT, 1], F32)
        ln_bnst = sb("ln_bnst", [P, TT, 4, 6], F32)
        ln_mv = sb("ln_mv", [P, TT, 2], F32)
        ln_t = sb("ln_t", [P, TT, 1], F32)
        ln_rstd = sb("ln_rstd", [P, TT, 1], F32)
        ln_nmr = sb("ln_nmr", [P, TT, 1], F32)
        id_sb = sb("id_sb", [P, P], BF16)
        ones_sb = sb("ones_sb", [P, P], BF16)
        gw_sb = sb("gw_sb", [P, P], BF16)
        wn_sb = sb("wn_sb", [P, KD + 1], F32)
        slots = [sb(f"wslot{i}", [P, SLOT_COLS], BF16) for i in range(RING)]

        ps = [es.enter_context(nc.psum_tensor(f"ps{i}", [P, CT], F32)) for i in range(6)]
        ps_trb = [es.enter_context(nc.psum_tensor(f"ps_tr{i}", [P, 8, P], BF16))
                  for i in range(2)]

        sem_names = ["xin", "xout", "wdma", "wb", "const", "cwn", "pe", "act", "dve"]
        sems = {n: es.enter_context(nc.semaphore("s_" + n)) for n in sem_names}

        stream = Stream(S, ws_d, wscr_d, slots, offsets, _plan_units(n_chunks, phases))

        S.op("gp", lambda e: e.dma_start(out=id_sb[:], in_=cb_d[:, 0:P]),
             inc="const", inc_by=16)
        S.op("gp", lambda e: e.dma_start(out=ones_sb[:], in_=cb_d[:, P:2 * P]),
             inc="const", inc_by=16)
        S.op("gp", lambda e: e.dma_start(out=gw_sb[:], in_=cb_d[:, 2 * P:3 * P]),
             inc="const", inc_by=16)
        S.op("sp", lambda e: e.dma_start(out=wn_sb[:], in_=wn_d),
             inc="cwn", inc_by=16)
        stream.prime()
        const_ev = ("const", S.cnt["const"])
        cwn_ev = ("cwn", 16)

        st = {
            "resid_dve": None,
            "resid_tt": [None] * TT,
            "xnT_evts": [],
            "sq_act": None,
            "tm_tr": [None, None],
                        "bank": [None] * 6,
            "tr_evac": [None, None],
            "cw_ev": None,
            "tr_bnk": 0,
            "ln_apply": {},
            "xnT_tt": [None] * TT,
            "xgy_pe": None,
            "sgr_dve": [None, None],
            "up_dve": [None, None],
            "ghat_pe": None,
            "a_read": [None, None], "ona_read": [None, None],
            "sgi_read": [None, None], "sgw_read": [None, None],
            "h_read": [None, None], "b_read": [None, None], "si_read": [None, None],
        }

        def emit_ln_q(c, site, tt, q, RS):
            deps = []
            if site == 1:
                deps.append(("xin", 16 * (c + 1)))
            S.op("dve", lambda e, tt=tt, q=q, RS=RS: e.bn_stats(
                ln_bnst[:, tt, q, :], RS[:, tt, q * 512:(q + 1) * 512]),
                waits=deps, inc="dve")

        def emit_ln_fin(c, site, tt, RS):
            S.op("dve", lambda e, tt=tt: e.bn_aggr(
                ln_mv[:, tt, :], ln_bnst[:, tt].rearrange("p a b -> p (a b)")),
                inc="dve")
            ev_d = ("dve", S.cnt["dve"])
            S.op("act", lambda e, tt=tt: e.activation(
                ln_t[:, tt, :], ln_mv[:, tt, 1:2], AF.Ln, bias=wn_sb[:, KD:KD + 1]),
                waits=[ev_d, cwn_ev], inc="act")
            ev_r = S.op("act", lambda e, tt=tt: e.activation(
                ln_rstd[:, tt, :], ln_t[:, tt, :], AF.Exp, scale=-0.5), inc="act")
            ev_nmr = S.op("dve", lambda e, tt=tt: e.tensor_scalar(
                ln_nmr[:, tt, :], ln_mv[:, tt, 0:1], ln_rstd[:, tt, :], -1.0,
                OP.mult, OP.mult),
                waits=[ev_r], inc="dve")
            st["ln_apply"][(site, tt)] = ev_nmr

        def emit_ln_stats_tt(c, site, tt, RS):
            for q in range(4):
                emit_ln_q(c, site, tt, q, RS)
            emit_ln_fin(c, site, tt, RS)

        def emit_ln(c, site, upto=3):
            RS = st["RS"]
            evts = []
            for tt in range(TT):
                if (site, tt) not in st["ln_apply"]:
                    emit_ln_stats_tt(c, site, tt, RS)
                ev_nmr = st["ln_apply"].pop((site, tt))
                par = tt % 2
                ev_ap_h = [None, None]
                ev_ap_h[0] = S.op("act", lambda e, tt=tt, par=par, RS=RS: e.activation(
                    xn_tm[:, par, 0:D // 2], RS[:, tt, 0:D // 2], AF.Identity,
                    bias=ln_nmr[:, tt, :], scale=ln_rstd[:, tt, :]),
                    waits=[ev_nmr, st["tm_tr"][par], st["resid_tt"][tt]], inc="act")
                ev_ap_h[1] = S.op("act", lambda e, tt=tt, par=par, RS=RS: e.activation(
                    xn_tm[:, par, D // 2:D], RS[:, tt, D // 2:D], AF.Identity,
                    bias=ln_nmr[:, tt, :], scale=ln_rstd[:, tt, :]),
                    waits=[ev_nmr], inc="act")
                for batch in range(KD // 8):
                    bnk = st["tr_bnk"]
                    st["tr_bnk"] = 1 - bnk
                    for j in range(8):
                        k = batch * 8 + j
                        w0 = [ev_ap_h[batch], const_ev]
                        if j == 0:
                            w0.append(st["tr_evac"][bnk])
                        ev_tr = S.op("pe", lambda e, k=k, j=j, bnk=bnk, par=par:
                                     e.transpose(ps_trb[bnk][:, j, :],
                                                 xn_tm[:, par, k * P:(k + 1) * P],
                                                 id_sb[:]),
                                     waits=w0, inc="pe" if j == 7 else None)
                    st["tm_tr"][par] = ev_tr
                    k0 = batch * 8
                    ev_ev = S.op("dve", lambda e, tt=tt, k0=k0, bnk=bnk:
                                 e.tensor_copy(xnT[:, k0:k0 + 8, tt * P:(tt + 1) * P],
                                               ps_trb[bnk][:]),
                                 waits=[ev_tr], inc="dve")
                    st["tr_evac"][bnk] = ev_ev
                    evts.append(ev_ev)
                st["xnT_tt"][tt] = ev_ev
            st["xnT_evts"] = [evts[-1]]

        def emit_router(c, call):
            for tt in range(TT):
                bk = 2 + tt
                for k in range(KD):
                    S.op("pe", lambda e, tt=tt, k=k, bk=bk: e.matmul(
                        ps[bk][:, 0:8], xnT[:, k, tt * P:(tt + 1) * P],
                        gw_sb[:, k * 8:(k + 1) * 8],
                        start=(k == 0), stop=(k == KD - 1),
                        skip_group_check=True),
                        waits=([st["xnT_tt"][tt], st["bank"][bk], const_ev]) if k == 0 else [],
                        inc="pe" if k == KD - 1 else None)
                ev_mm = ("pe", S.cnt["pe"])
                ev_cp = S.op("dve", lambda e, tt=tt, bk=bk: e.tensor_copy(rt[:, tt, :], ps[bk][:, 0:8]),
                             waits=[ev_mm], inc="dve")
                st["bank"][bk] = ev_cp
            ev_r = S.op("dve", lambda e: e.reduce_max(rmx[:], rt[:], axis=AX.X), inc="dve")
            S.selfwait("dve", ev_r)
            S.op("dve", lambda e: e.tensor_single_scalar(rnmx[:], rmx[:], -1.0, OP.mult),
                 inc="dve")
            ev_n = ("dve", S.cnt["dve"])
            exev = []
            for tt in range(TT):
                exev.append(S.op("act", lambda e, tt=tt: e.activation(
                    rex[:, tt, :], rt[:, tt, :], AF.Exp, bias=rnmx[:, tt, :]),
                    waits=[ev_n], inc="act"))
            ev_r = S.op("dve", lambda e: e.reduce_sum(rz[:], rex[:], axis=AX.X),
                 waits=[exev[-1]], inc="dve")
            S.selfwait("dve", ev_r)
            S.op("dve", lambda e: e.reciprocal(rzi[:], rz[:]), inc="dve")
            for tt in range(TT):
                S.op("dve", lambda e, tt=tt: e.tensor_single_scalar(
                    rs[:, tt, :], rex[:, tt, :], rzi[:, tt, :], OP.mult), inc="dve")
            ev_r = S.op("dve", lambda e: e.reduce_max(rm1[:], rs[:], axis=AX.X), inc="dve")
            S.selfwait("dve", ev_r)
            for tt in range(TT):
                S.op("dve", lambda e, tt=tt: e.tensor_single_scalar(
                    rmask[:, tt, :], rs[:, tt, :], rm1[:, tt, :], OP.is_ge), inc="dve")
                S.op("dve", lambda e, tt=tt: e.scalar_tensor_tensor(
                    rex[:, tt, :], rmask[:, tt, :], -1e30, rs[:, tt, :],
                    OP.mult, OP.add), inc="dve")
            ev_r = S.op("dve", lambda e: e.reduce_max(rm2[:], rex[:], axis=AX.X), inc="dve")
            S.selfwait("dve", ev_r)
            for tt in range(TT):
                S.op("dve", lambda e, tt=tt: e.tensor_single_scalar(
                    rmask[:, tt, :], rs[:, tt, :], rm2[:, tt, :], OP.is_ge), inc="dve")
            S.op("dve", lambda e: e.tensor_add(rden[:], rm1[:], rm2[:]), inc="dve")
            S.op("dve", lambda e: e.tensor_single_scalar(rden[:], rden[:], 1e-20, OP.add),
                 inc="dve")
            S.op("dve", lambda e: e.reciprocal(rdi[:], rden[:]), inc="dve")
            S.op("dve", lambda e: e.tensor_single_scalar(rdi[:], rdi[:], 0.5, OP.mult),
                 inc="dve")
            for tt in range(TT):
                S.op("dve", lambda e, tt=tt: e.tensor_mul(
                    rex[:, tt, :], rs[:, tt, :], rmask[:, tt, :]), inc="dve")
                ev = S.op("dve", lambda e, tt=tt: e.tensor_single_scalar(
                    cw[:, tt, :], rex[:, tt, :], rdi[:, tt, :], OP.mult), inc="dve")
            st["cw_ev"] = ev

        def emit_expert(c, call, u, tail_hook=None):
            gh_evts = []
            slot_info = None
            ev_g = {}
            ev_up_copy = {-1: None}
            for fc in range(FKC):
                j = fc // 2
                fl = fc % 2
                if fl == 0:
                    g_id, slot, ready = stream.consume(("gu", u, j))
                    slot_info = (g_id, slot, ready)
                g_id, slot, ready = slot_info
                gb = fc % 2

                def gtile(slot, fl, gu, k):
                    base = (fl * 2 + gu) * 2048 + k * P
                    return slot[:, base:base + P]

                for k in range(KD):
                    w0 = []
                    if k == 0:
                        w0 = st["xnT_evts"] + [ready, st["bank"][0]]
                    S.op("pe", lambda e, slot=slot, fl=fl, k=k: e.matmul(
                        ps[0][:], gtile(slot, fl, 0, k), xnT[:, k, :],
                        start=(k == 0), stop=(k == KD - 1)),
                        waits=w0, inc="pe" if k == KD - 1 else None)
                ev_gate = ("pe", S.cnt["pe"])
                for k in range(KD):
                    w0 = [ev_up_copy[fc - 1], st["bank"][1]] if k == 0 else []
                    S.op("pe", lambda e, slot=slot, fl=fl, k=k: e.matmul(
                        ps[1][:], gtile(slot, fl, 1, k), xnT[:, k, :],
                        start=(k == 0), stop=(k == KD - 1)),
                        waits=w0, inc="pe" if k == KD - 1 else None)
                ev_up = ("pe", S.cnt["pe"])
                if fc == FKC - 1:
                    stream.release(g_id, ev_up)
                elif fl == 1:
                    stream.release(g_id, ev_up)
                ev_si = S.op("act", lambda e, gb=gb: e.activation(
                    sgr[:, gb, :], ps[0][:], AF.Silu),
                    waits=[ev_gate, st["sgr_dve"][gb]], inc="act")
                st["bank"][0] = ev_si
                ev_uc = S.op("act", lambda e, gb=gb: e.copy(uprot[:, gb, :], ps[1][:]),
                             waits=[ev_up, st["up_dve"][gb]], inc="act")
                ev_up_copy[fc] = ev_uc
                st["bank"][1] = ev_uc
                ev_gh = S.op("dve", lambda e, fc=fc, gb=gb: e.tensor_mul(
                    ghat[:, fc, :], sgr[:, gb, :], uprot[:, gb, :]),
                    waits=[ev_si, ev_uc, st["ghat_pe"]], inc="dve")
                st["sgr_dve"][gb] = ev_gh
                st["up_dve"][gb] = ev_gh
                gh_evts.append(ev_gh)
            ghat_ready = gh_evts[-1]

            RS = st["RS"]
            for hq in range(HQN):
                g_id, slot, ready = stream.consume(("wd", u, hq))
                last_grp = None
                for tt in range(TT):
                    bk = 2 + tt
                    for f in range(FKC):
                        w0 = [gh_evts[f]]
                        if f == 0:
                            w0 += [ready, st["bank"][bk]]
                        S.op("pe", lambda e, slot=slot, f=f, tt=tt, bk=bk: e.matmul(
                            ps[bk][:], ghat[:, f, tt * P:(tt + 1) * P],
                            slot[:, f * 512:(f + 1) * 512],
                            start=(f == 0), stop=(f == FKC - 1)),
                            waits=w0, inc="pe" if f == FKC - 1 else None)
                    ev_grp = ("pe", S.cnt["pe"])
                    last_grp = ev_grp
                    if u < 2:
                        cwsc = 0.5
                    else:
                        cwsc = cw[:, tt, (u - 2):(u - 1)]
                    ev_ev = S.op("dve", lambda e, tt=tt, hq=hq, bk=bk, cwsc=cwsc, RS=RS:
                                 e.scalar_tensor_tensor(
                                     RS[:, tt, hq * 512:(hq + 1) * 512], ps[bk][:],
                                     cwsc, RS[:, tt, hq * 512:(hq + 1) * 512],
                                     OP.mult, OP.add),
                                 waits=[ev_grp, st["cw_ev"]], inc="dve")
                    st["bank"][bk] = ev_ev
                    st["resid_dve"] = ev_ev
                    st["resid_tt"][tt] = ev_ev
                    if tail_hook is not None:
                        tail_hook(tt, hq)
                stream.release(g_id, last_grp)
            st["ghat_pe"] = last_grp

        def emit_hgrn(c, tail_hook=None):
            RS, XG = st["RS"], st["XG"]
            sq_evts = []
            for et in range(KD):
                par = et % 2
                g_id, slot, ready = stream.consume(("ifg", et, 0))
                pe_ev = {}
                for mi, bk in ((0, 0), (1, 1), (2, 2)):
                    for k in range(KD):
                        w0 = []
                        if k == 0:
                            w0 = st["xnT_evts"] + [ready, st["bank"][bk]]
                        S.op("pe", lambda e, slot=slot, mi=mi, k=k, bk=bk: e.matmul(
                            ps[bk][:], slot[:, (mi * KD + k) * P:(mi * KD + k + 1) * P],
                            xnT[:, k, :],
                            start=(k == 0), stop=(k == KD - 1)),
                            waits=w0, inc="pe" if k == KD - 1 else None)
                    pe_ev[mi] = ("pe", S.cnt["pe"])
                stream.release(g_id, pe_ev[2])
                ev_a = S.op("act", lambda e, par=par: e.activation(
                    a_rot[:, par, :], ps[0][:], AF.Sigmoid),
                    waits=[pe_ev[0], st["a_read"][par]], inc="act")
                ev_ona = S.op("act", lambda e, par=par: e.activation(
                    ona_rot[:, par, :], ps[0][:], AF.Sigmoid, scale=-1.0),
                    waits=[st["ona_read"][par]], inc="act")
                st["bank"][0] = ev_ona
                ev_sgi = S.op("act", lambda e, par=par: e.activation(
                    sgi_rot[:, par, :], ps[1][:], AF.Sigmoid),
                    waits=[pe_ev[1], st["sgi_read"][par]], inc="act")
                ev_sgg = S.op("act", lambda e, par=par: e.activation(
                    sgw_rot[:, par, :], ps[2][:], AF.Sigmoid),
                    waits=[pe_ev[2], st["sgw_read"][par]], inc="act")
                ev_si = S.op("dve", lambda e, par=par: e.tensor_mul(
                    si_rot[:, par, :], sgi_rot[:, par, :], ps[1][:]),
                    waits=[ev_sgi, st["si_read"][par]], inc="dve")
                st["sgi_read"][par] = ev_si
                st["bank"][1] = ev_si
                ev_b = S.op("dve", lambda e, par=par: e.tensor_mul(
                    b_rot[:, par, :], si_rot[:, par, :], ona_rot[:, par, :]),
                    waits=[ev_ona, st["b_read"][par]], inc="dve")
                st["si_read"][par] = ev_b
                st["ona_read"][par] = ev_b
                ev_sgw = S.op("dve", lambda e, par=par, et=et: e.scalar_tensor_tensor(
                    sgw_rot[:, par, :], sgw_rot[:, par, :], wn_sb[:, et:et + 1],
                    ps[2][:], OP.mult, OP.mult),
                    waits=[ev_sgg, cwn_ev], inc="dve")
                st["bank"][2] = ev_sgw
                init = 0.0 if c == 0 else carry[:, et:et + 1]
                ev_h = S.op("dve", lambda e, par=par, init=init: e.tensor_tensor_scan(
                    h_rot[:, par, :], a_rot[:, par, :], b_rot[:, par, :], init,
                    OP.mult, OP.add),
                    waits=[ev_a, st["h_read"][par]], inc="dve")
                st["a_read"][par] = ev_h
                st["b_read"][par] = ev_h
                S.op("dve", lambda e, et=et, par=par: e.tensor_copy(
                    carry[:, et:et + 1], h_rot[:, par, CT - 1:CT]), inc="dve")
                ev_xg = S.op("dve", lambda e, et=et, par=par, XG=XG: e.tensor_mul(
                    XG[:, et, :], h_rot[:, par, :], sgw_rot[:, par, :]),
                    waits=[st["sq_act"], ("xout", 16 * c)], inc="dve")
                st["h_read"][par] = ev_xg
                st["sgw_read"][par] = ev_xg
                ev_sq = S.op("act", lambda e, et=et, XG=XG: e.activation(
                    xgy[:, et, :], XG[:, et, :], AF.Square),
                    waits=[ev_xg, st["xgy_pe"]], inc="act")
                sq_evts.append(ev_sq)
            st["sq_act"] = sq_evts[-1]

            for et in range(KD):
                S.op("pe", lambda e, et=et: e.matmul(
                    ps[2][:], ones_sb[:], xgy[:, et, :],
                    start=(et == 0), stop=(et == KD - 1)),
                    waits=[sq_evts[et], st["bank"][2]] if et == 0 else [sq_evts[et]],
                    inc="pe" if et == KD - 1 else None)
            ev_ones = ("pe", S.cnt["pe"])
            S.op("dve", lambda e: e.tensor_single_scalar(r_tmp[:], ps[2][:], 1.0 / D, OP.mult),
                 waits=[ev_ones], inc="dve")
            ev_rt = ("dve", S.cnt["dve"])
            st["bank"][2] = ev_rt
            S.op("act", lambda e: e.activation(r_tmp[:], r_tmp[:], AF.Ln, bias=wn_sb[:, KD:KD + 1]),
                 waits=[ev_rt, cwn_ev], inc="act")
            ev_rb = S.op("act", lambda e: e.activation(r_b[:], r_tmp[:], AF.Exp, scale=-0.5),
                         inc="act")
            y_evts = []
            for et in range(KD):
                y_evts.append(S.op("dve", lambda e, et=et, XG=XG: e.tensor_mul(
                    xgy[:, et, :], XG[:, et, :], r_b[:]),
                    waits=[ev_rb, ev_ones], inc="dve"))
            st["xload_deps"] = [y_evts[-1], sq_evts[-1]]
            last_grp = None
            for dq in range(4):
                g_id, slot, ready = stream.consume(("wo", dq, 0))
                for et in range(KD):
                    for tt in range(TT):
                        bk = 2 + tt
                        w0 = [y_evts[et]]
                        if et == 0:
                            w0 = [ready, y_evts[et], st["bank"][bk]]
                        S.op("pe", lambda e, slot=slot, et=et, tt=tt, bk=bk: e.matmul(
                            ps[bk][:], xgy[:, et, tt * P:(tt + 1) * P],
                            slot[:, et * 512:(et + 1) * 512],
                            start=(et == 0), stop=(et == KD - 1),
                            skip_group_check=True),
                            waits=w0, inc="pe" if (et == KD - 1) else None)
                        if et == KD - 1:
                            ev_grp = ("pe", S.cnt["pe"])
                            ev_ev = S.op("dve", lambda e, tt=tt, dq=dq, bk=bk, RS=RS:
                                         e.tensor_tensor(
                                             RS[:, tt, dq * 512:(dq + 1) * 512],
                                             RS[:, tt, dq * 512:(dq + 1) * 512],
                                             ps[bk][:], OP.add),
                                         waits=[ev_grp], inc="dve")
                            st["bank"][bk] = ev_ev
                            st["resid_dve"] = ev_ev
                            st["resid_tt"][tt] = ev_ev
                            last_grp = ev_grp
                            if tail_hook is not None:
                                tail_hook(tt, dq)
                stream.release(g_id, last_grp)
            st["xgy_pe"] = last_grp

        st["xload_deps"] = []
        for c in range(n_chunks):
            RS = rbuf[c % 2][:].rearrange("p (a b) -> p a b", a=TT)
            XG = rbuf[1 - (c % 2)][:].rearrange("p (a b) -> p a b", a=KD)
            st["RS"], st["XG"] = RS, XG
            if c == 0 or phases != "full":
                w_load = ([("xout", 16 * c)] if c > 0 else []) + st["xload_deps"]
                S.op("sp", lambda e, c=c, RS=RS: e.dma_start(out=RS, in_=x_r[c]),
                     waits=w_load, inc="xin", inc_by=16)
            if phases == "copy":
                S.op("sp", lambda e, c=c, RS=RS: e.dma_start(out=y_r[c], in_=RS),
                     waits=[("xin", 16 * (c + 1))], inc="xout", inc_by=16)
                continue
            emit_ln(c, 1, upto=LN_UPTO)
            if debug and c == 0 and phases in ("ln1", "ffn1"):
                S.op("sp", lambda e: e.dma_start(out=dbg_xnT, in_=xnT[:]),
                     waits=st["xnT_evts"], inc="xout", inc_by=16)
            if phases in ("ffn1", "attn", "full"):
                emit_router(c, 0)
                if debug and c == 0:
                    S.op("sp", lambda e: e.dma_start(out=dbg_cw, in_=cw[:]),
                         waits=[st["cw_ev"]], inc="xout", inc_by=16)
                def hook2(tt, q, c=c, RS=RS):
                    emit_ln_q(c, 2, tt, q, RS)
                    if q == 3:
                        emit_ln_fin(c, 2, tt, RS)
                if phases not in ("attn", "full"):
                    hook2 = None
                for u in range(NU):
                    emit_expert(c, 0, u, tail_hook=hook2 if u == NU - 1 else None)
            if phases in ("attn", "full"):
                emit_ln(c, 2)
                def hook3(tt, q, c=c, RS=RS):
                    emit_ln_q(c, 3, tt, q, RS)
                    if q == 3:
                        emit_ln_fin(c, 3, tt, RS)
                if phases != "full":
                    hook3 = None
                emit_hgrn(c, tail_hook=hook3)
                if phases == "full" and c + 1 < n_chunks:
                    nxt = rbuf[(c + 1) % 2][:].rearrange("p (a b) -> p a b", a=TT)
                    S.op("sp", lambda e, c=c, nxt=nxt: e.dma_start(
                        out=nxt, in_=x_r[c + 1]),
                        waits=st["xload_deps"], inc="xin", inc_by=16)
            if phases == "full":
                emit_ln(c, 3)
                emit_router(c, 1)
                if c + 1 < n_chunks:
                    RSn = rbuf[(c + 1) % 2][:].rearrange("p (a b) -> p a b", a=TT)
                    def hook1(tt, q, c=c, RSn=RSn):
                        emit_ln_q(c + 1, 1, tt, q, RSn)
                        if q == 3:
                            emit_ln_fin(c + 1, 1, tt, RSn)
                else:
                    hook1 = None
                for u in range(NU):
                    emit_expert(c, 1, u, tail_hook=hook1 if u == NU - 1 else None)
            S.op("sp", lambda e, c=c, RS=RS: e.dma_start(out=y_r[c], in_=RS),
                 waits=[st["resid_dve"], ("dve", S.cnt["dve"])], inc="xout", inc_by=16)

        final_out = ("xout", S.cnt["xout"])
        S.op("sp", None, waits=[final_out])

        with nc.Block() as block:
            @block.sync
            def _(eng):
                S.replay("sp", eng, sems)

            @block.gpsimd
            def _(eng):
                S.replay("gp", eng, sems)

            @block.tensor
            def _(eng):
                S.replay("pe", eng, sems)

            @block.scalar
            def _(eng):
                S.replay("act", eng, sems)

            @block.vector
            def _(eng):
                S.replay("dve", eng, sems)

    return nc


def kernel(L0_attn_x, L0_attn_Wi, L0_attn_Wf, L0_attn_Wg, L0_attn_w_norm,
           L0_attn_Wo, L0_ffn_gate_weight, L0_ffn_w_gate, L0_ffn_w_up,
           L0_ffn_w_down, L0_ffn_sw_gate, L0_ffn_sw_up, L0_ffn_sw_down):
    inp = dict(L0_attn_Wi=L0_attn_Wi, L0_attn_Wf=L0_attn_Wf, L0_attn_Wg=L0_attn_Wg,
               L0_attn_w_norm=L0_attn_w_norm, L0_attn_Wo=L0_attn_Wo,
               L0_ffn_gate_weight=L0_ffn_gate_weight, L0_ffn_w_gate=L0_ffn_w_gate,
               L0_ffn_w_up=L0_ffn_w_up, L0_ffn_w_down=L0_ffn_w_down,
               L0_ffn_sw_gate=L0_ffn_sw_gate, L0_ffn_sw_up=L0_ffn_sw_up,
               L0_ffn_sw_down=L0_ffn_sw_down)
    wstream, cb, wn = _pack_host(inp)
    x = np.asarray(L0_attn_x, dtype=np.float32)
    nc = build_program()
    in_maps = [dict(x=np.ascontiguousarray(x[b]), wstream=wstream, cb=cb, wn=wn)
               for b in range(8)]
    res = run_bass_kernel_spmd(nc, in_maps, core_ids=list(range(8)))
    return np.stack([r["y"] for r in res.results], axis=0)
